# version 01
# baseline (speedup 1.0000x reference)
import contextlib
import numpy as np
import ml_dtypes
import concourse.bass as bass
import concourse.mybir as mybir
from concourse.bass_utils import run_bass_kernel_spmd
from concourse.alu_op_type import AluOpType as OP

F32 = mybir.dt.float32
F32R = mybir.dt.float32r
BF16 = mybir.dt.bfloat16
AF = mybir.ActivationFunctionType
AX = mybir.AxisListType

D = 1024
DH = 512
NHEAD = 8
HD = 64
NE = 8
NST = 16

PHASES = "ABCDEFG"
CONST_VALS = [1e-5, -0.5, -4.0, 0.125, 1.0, 0.0, -1.0, 2.0]


class Sem:
    def __init__(self, nc, name):
        self.h = nc.alloc_semaphore(name)
        self.v = 0

    def inc(self, instr, n=1):
        instr.then_inc(self.h, n)
        self.v += n
        return self.v

    def dma(self, instr):
        return self.inc(instr, 16)


def _mkview(base, byte_off, shape, dt):
    ap = base[:, byte_off // 2:]
    ap = ap.bitcast(dt)
    n = int(np.prod(shape[1:]))
    ap = ap[0:shape[0], 0:n]
    if len(shape) == 3:
        ap = ap.rearrange("p (a b) -> p a b", b=shape[2])
    return ap


def build_nc(S=4096, phases=PHASES, debug=False, bparts="123"):
    NTC = S // 512
    NT32 = S // 128

    nc = bass.Bass("TRN2", target_bir_lowering=False)

    def din(name, shape, dt=F32):
        return nc.dram_tensor(name, list(shape), dt, kind="ExternalInput").ap()

    x_d = din("x", (S, D))
    identf_d = din("identf", (128, 128))
    identb_d = din("identb", (128, 128), BF16)
    onesb_d = din("onesb", (1, 128), BF16)
    r1_d = din("r1", (128, 512))
    zred_d = din("zred", (128, 2048), BF16)
    alog_d = din("alog", (128, 64))
    dskip_d = din("dskip", (128, 4))
    wd_d = din("wd", (128, 8, DH))
    bd_d = din("bd", (128, 4))
    wb_d = din("wb", (128, 8, NST))
    bb_d = din("bb", (NST, 1))
    wc_d = din("wc", (128, 8, NST))
    bc_d = din("bc", (NST, 1))
    wg_d = din("wg", (128, 8, NE))
    bg_d = din("bg", (NE, 1))
    wq_d = din("wq", (128, 8, DH))
    wk_d = din("wk", (128, 8, DH))
    wv_d = din("wv", (128, 8, DH))
    bq_d = din("bq", (128, 4))
    bk_d = din("bk", (128, 4))
    bv_d = din("bv", (128, 4))
    wexp_d = din("wexp", (NE, 128, 8, 4, 128))
    bexp_d = din("bexp", (NE, DH))
    cvals_d = din("cvals", (128, 8))

    out_d = nc.dram_tensor("out", [128, 4 * S], F32, kind="ExternalOutput").ap()

    def dscr(name, shape, dt=F32):
        return nc.dram_tensor(name, list(shape), dt).ap()

    xnup_d = dscr("xnup_s", (128, 4 * S), BF16)
    comb_d = dscr("comb_s", (NE, S))
    bT_d = dscr("bT_s", (NST, S))
    cT_d = dscr("cT_s", (NST, S))
    qt_d = dscr("qt_s", (128, 4 * S), BF16)
    den_d = dscr("den_s", (64, 512))
    rden_d = dscr("rden_s", (64, 512))

    sb = lambda n, s, d=F32: nc.alloc_sbuf_tensor(n, list(s), d)
    out_acc = sb("out_acc", (128, 4 * S))
    xnlow = sb("xnlow", (128, max(4 * S, 16384)), BF16)
    KT = sb("KT", (128, max(4 * S, 16384)), BF16)
    vtok = sb("vtok", (128, max(NT32 * 520, 16384)), BF16)
    qt_pair = sb("qt_pair", (128, S), BF16)
    identf = sb("sb_identf", (128, 128))
    identb = sb("sb_identb", (128, 128), BF16)
    onesb = sb("sb_onesb", (1, 128), BF16)
    r1 = sb("sb_r1", (128, 512))
    zred = sb("sb_zred", (128, 2048), BF16)
    alog = sb("sb_alog", (128, 64))
    a_il = sb("a_il", (128, 64))
    dskip = sb("sb_dskip", (128, 4))
    bd = sb("sb_bd", (128, 4))
    bbv = sb("bbv", (NST, 1))
    bcv = sb("bcv", (NST, 1))
    bgv = sb("bgv", (NE, 1))
    bq = sb("sb_bq", (128, 4))
    bk = sb("sb_bk", (128, 4))
    bv = sb("sb_bv", (128, 4))
    state = sb("state", (128, 64))
    wbig = sb("wbig", (128, 8 * DH), BF16)
    wb_bf = sb("wb_bf", (128, 8 * NST), BF16)
    wc_bf = sb("wc_bf", (128, 8 * NST), BF16)
    wg_bf = sb("wg_bf", (128, 8 * NE), BF16)
    bexp_bf = sb("bexp_bf", (NE, DH), BF16)
    wexp_bf = [sb(f"wexp_bf{i}", (128, 4096), BF16) for i in range(2)]
    cbrow = [sb(f"cbrow{i}", (NE, 512), BF16) for i in range(2)]
    srow = sb("srow", (1, 1024), BF16)
    cvals = sb("cvals_sb", (128, 8))
    for ci, cv in enumerate(CONST_VALS):
        nc.const_aps.register(F32, cv, cvals[:, ci:ci + 1])

    kview = lambda o, s, d: _mkview(KT, o, s, d)
    xview = lambda o, s, d: _mkview(xnlow, o, s, d)
    vview = lambda o, s, d: _mkview(vtok, o, s, d)

    xn_lo = xnlow[:, 0:4 * S].rearrange("p (f t) -> p f t", t=S)
    oa = out_acc[:].rearrange("p (g t) -> p g t", t=S)
    xnup_dv = xnup_d.rearrange("p (f t) -> p f t", t=S)
    KT_v = KT[:, 0:4 * S].rearrange("p (pg t) -> p pg t", t=S)
    vt_v = vtok[:, 0:NT32 * 520].rearrange("p (tt c) -> p tt c", c=520)
    vt_h = vtok[:, 0:NT32 * 520].rearrange("p (tt h j) -> p tt h j", h=8, j=65)
    qt_dv = qt_d.rearrange("p (pg t) -> p pg t", t=S)

    s_ld = Sem(nc, "s_ld")
    s_gld = Sem(nc, "s_gld")
    s_pe = Sem(nc, "s_pe")
    s_dv = Sem(nc, "s_dv")
    s_ac = Sem(nc, "s_ac")
    s_po = Sem(nc, "s_po")
    s_st = Sem(nc, "s_st")

    with nc.Block() as blk:
        @blk.sync
        def _(q):
            for dst, src in [
                (identf[:], identf_d), (identb[:], identb_d), (onesb[:], onesb_d),
                (r1[:], r1_d), (zred[:], zred_d), (alog[:], alog_d),
                (dskip[:], dskip_d), (bd[:], bd_d), (bbv[:], bb_d), (bcv[:], bc_d),
                (bgv[:], bg_d), (bq[:], bq_d), (bk[:], bk_d), (bv[:], bv_d),
                (cvals[:], cvals_d),
            ]:
                s_ld.dma(q.dma_start(out=dst, in_=src))

        consts_ld = s_ld.v

        @blk.scalar
        def _(a):
            a.wait_ge(s_ld.h, consts_ld)
            s_ac.inc(nc.scalar.activation(a_il[:], alog[:], AF.Exp, bias=0.0, scale=1.0))

        ail_ac = s_ac.v

        @blk.vector
        def _(v):
            v.wait_ge(s_ac.h, ail_ac)
            s_dv.inc(nc.vector.tensor_scalar(a_il[:], a_il[:], -1.0, None, op0=OP.mult))

        if "A" in phases:
            import os as _os
            SERIAL = _os.environ.get("SERIAL", "0") == "1"

            def allwait(eng):
                if SERIAL:
                    eng.wait_ge(s_ld.h, s_ld.v)
                    eng.wait_ge(s_gld.h, s_gld.v)
                    eng.wait_ge(s_pe.h, s_pe.v)
                    eng.wait_ge(s_dv.h, s_dv.v)
                    eng.wait_ge(s_ac.h, s_ac.v)
                    eng.wait_ge(s_st.h, s_st.v)
            if _os.environ.get("DEDIC", "0") == "1":
                xtile = [sb("ded_xt0", (128, 1024), F32)[:], sb("ded_xt1", (128, 1024), F32)[:]]
                sqbuf = sb("ded_sq", (128, 1024), F32)[:]
                xstage = sb("ded_xs", (128, 512), BF16)[:]
                stats = [sb("ded_st0", (128, 12), F32)[:], sb("ded_st1", (128, 12), F32)[:]]
                xnbuf = sb("ded_xn", (128, 1024), BF16)[:]
            else:
                xtile = [kview(0, (128, 1024), F32), kview(4096, (128, 1024), F32)]
                sqbuf = kview(8192, (128, 1024), F32)
                xstage = kview(12288, (128, 512), BF16)
                stats = [kview(13312, (128, 12), F32), kview(13360, (128, 12), F32)]
                xnbuf = kview(13440, (128, 1024), BF16)
            _ps = contextlib.ExitStack()
            psT = _ps.enter_context(nc.psum_tensor("ps_auto1", [128, 1024], BF16))[:]
            psX = _ps.enter_context(nc.psum_tensor("ps_auto2", [128, 512], F32))[:]
            pe_done = []

            for i in range(NT32):
                par = i % 2
                st_ = stats[par]

                @blk.sync
                def _(q, i=i, par=par):
                    if i >= 2:
                        q.wait_ge(s_pe.h, pe_done[i - 2])
                    s_ld.dma(q.dma_start(out=xtile[par], in_=x_d[i * 128:(i + 1) * 128, :]))

                ld_here = s_ld.v

                @blk.vector
                def _(v, i=i, par=par, st_=st_):
                    allwait(v)
                    v.wait_ge(s_ld.h, ld_here)
                    xt = xtile[par]
                    nc.vector.tensor_reduce(st_[:, 0:1], xt, axis=AX.X, op=OP.add)
                    nc.vector.tensor_tensor(sqbuf[:], xt, xt, op=OP.mult)
                    s_dv.inc(nc.vector.tensor_reduce(st_[:, 1:2], sqbuf[:], axis=AX.X, op=OP.add))
                    v.wait_ge(s_dv.h, s_dv.v)
                    nc.vector.tensor_scalar(st_[:, 2:3], st_[:, 0:1], 1.0 / D, None, op0=OP.mult)
                    s_dv.inc(nc.vector.tensor_scalar(st_[:, 3:4], st_[:, 1:2], 1.0 / D, None, op0=OP.mult))
                    v.wait_ge(s_dv.h, s_dv.v)
                    s_dv.inc(nc.vector.scalar_tensor_tensor(
                        st_[:, 5:6], st_[:, 2:3], st_[:, 2:3], st_[:, 3:4],
                        op0=OP.mult, op1=OP.subtract))

                var_dv = s_dv.v

                @blk.scalar
                def _(a, st_=st_):
                    allwait(a)
                    a.wait_ge(s_dv.h, var_dv)
                    s_ac.inc(nc.scalar.activation(st_[:, 6:7], st_[:, 5:6], AF.Ln, bias=1e-5, scale=-1.0))
                    a.wait_ge(s_ac.h, s_ac.v)
                    s_ac.inc(nc.scalar.activation(st_[:, 7:8], st_[:, 6:7], AF.Exp, bias=0.0, scale=-0.5))

                rstd_ac = s_ac.v

                @blk.vector
                def _(v, par=par, st_=st_):
                    allwait(v)
                    v.wait_ge(s_ac.h, rstd_ac)
                    v.wait_ge(s_pe.h, s_pe.v)
                    s_dv.inc(nc.vector.tensor_tensor(
                        sqbuf[:], xtile[par], st_[:, 2:3].broadcast_to([128, 1024]),
                        op=OP.subtract))
                    v.wait_ge(s_dv.h, s_dv.v)
                    s_dv.inc(nc.vector.tensor_tensor(
                        xnbuf[:], sqbuf[:], st_[:, 7:8].broadcast_to([128, 1024]),
                        op=OP.mult))

                xn_dv = s_dv.v

                @blk.tensor
                def _(t, par=par):
                    allwait(t)
                    t.wait_ge(s_dv.h, xn_dv)
                    t.wait_ge(s_dv.h, s_dv.v)
                    for fc in range(8):
                        instr = nc.tensor.transpose(psT[:, fc * 128:(fc + 1) * 128],
                                                    xnbuf[:, fc * 128:(fc + 1) * 128], identb[:])
                    for g in range(4):
                        instr = nc.tensor.transpose(psX[:, g * 128:(g + 1) * 128],
                                                    xtile[par][:, g * 128:(g + 1) * 128], identf[:])
                    s_pe.inc(instr)

                pe_done.append(s_pe.v)
                tp_pe = s_pe.v

                if debug and i < 8:
                    dbg_xnb = nc.dram_tensor(f"dbg_xnb{i}", [128, 1024], BF16,
                                             kind="ExternalOutput").ap()
                    dbg_stats = nc.dram_tensor(f"dbg_stats{i}", [128, 12], F32,
                                               kind="ExternalOutput").ap()

                    @blk.sync
                    def _(q, i=i, st_=st_):
                        q.wait_ge(s_pe.h, tp_pe)
                        s_st.dma(q.dma_start(out=dbg_xnb, in_=xnbuf[:]))
                        s_st.dma(q.dma_start(out=dbg_stats, in_=st_))

                    st_snap = s_st.v

                @blk.vector
                def _(v, i=i):
                    allwait(v)
                    if debug and i < 8:
                        v.wait_ge(s_st.h, st_snap)
                    v.wait_ge(s_pe.h, tp_pe)
                    if i >= 1:
                        v.wait_ge(s_st.h, s_st.v)
                    nc.vector.tensor_copy(
                        xn_lo[:, :, i * 128:(i + 1) * 128],
                        psT[:, 0:512].rearrange("p (f t) -> p f t", t=128))
                    nc.vector.tensor_copy(xstage[:], psT[:, 512:1024])
                    s_dv.inc(nc.vector.tensor_copy(
                        oa[:, :, i * 128:(i + 1) * 128],
                        psX[:].rearrange("p (g t) -> p g t", t=128)))

                ev_dv = s_dv.v

                @blk.sync
                def _(q, i=i):
                    q.wait_ge(s_dv.h, ev_dv)
                    s_st.dma(q.dma_start(
                        out=xnup_dv[:, :, i * 128:(i + 1) * 128],
                        in_=xstage[:].rearrange("p (f t) -> p f t", t=128)))

    try:
        _ps.close()
    except NameError:
        pass

    with nc.Block() as blk:
        @blk.gpsimd
        def _(g):
            for dst, src in [
                (wb_bf[:], wb_d), (wc_bf[:], wc_d),
                (wg_bf[:], wg_d), (bexp_bf[:], bexp_d),
            ]:
                s_gld.dma(g.dma_start(out=dst, in_=src))

    wcast_gld = s_gld.v
    xnup_done = s_st.v

    if debug:
        with nc.Block() as blk:
            dbg_xnlow = nc.dram_tensor("dbg_xnlow", [128, 4 * S], BF16, kind="ExternalOutput").ap()
            dbg_oa1 = nc.dram_tensor("dbg_oa1", [128, 4 * S], F32, kind="ExternalOutput").ap()

            @blk.sync
            def _(q):
                q.wait_ge(s_dv.h, s_dv.v)
                s_st.dma(q.dma_start(out=dbg_xnlow, in_=xnlow[:, 0:4 * S]))
                s_st.dma(q.dma_start(out=dbg_oa1, in_=out_acc[:]))
                q.wait_ge(s_st.h, s_st.v)
    xnup_done = s_st.v

    if "B" in phases:
        with nc.Block() as blk:
            xup = [kview(0, (128, 4, 512), BF16), kview(4096, (128, 4, 512), BF16)]
            zstage = kview(8192, (8, 512), F32)
            zt_sb = kview(10240, (128, 4, 8), F32)
            combT = kview(10368, (128, 4, 8), F32)
            cstage = kview(10496, (8, 512), F32)
            bstage = kview(12544, (16, 512), F32)
            cstage2 = kview(14592, (16, 512), F32)
            gtmp = kview(16640, (128, 4, 16), F32)
            egbuf = kview(16896 + 128, (128, 4, 8), F32)
            _psB = contextlib.ExitStack()
            psZ = _psB.enter_context(nc.psum_tensor("ps_auto3", [16, 512], F32))[:]
            psZT = _psB.enter_context(nc.psum_tensor("ps_auto4", [128, 32], F32))[:]
            psBC = _psB.enter_context(nc.psum_tensor("ps_auto5", [16, 1024], F32))[:]
            psCT = _psB.enter_context(nc.psum_tensor("ps_auto6", [8, 512], F32))[:]
            pe_done = []

            for tc in range(NTC):
                par = tc % 2

                @blk.gpsimd
                def _(g, tc=tc, par=par):
                    if tc == 0:
                        g.wait_ge(s_st.h, xnup_done)
                    if tc >= 2:
                        g.wait_ge(s_pe.h, pe_done[3 * (tc - 2)])
                    s_gld.dma(g.dma_start(out=xup[par],
                                          in_=xnup_dv[:, :, tc * 512:(tc + 1) * 512]))

                ld_here = s_gld.v

                @blk.tensor
                def _(t, tc=tc, par=par):
                    t.wait_ge(s_gld.h, ld_here)
                    if tc == 0:
                        t.wait_ge(s_ld.h, s_ld.v)
                    t.wait_ge(s_dv.h, s_dv.v)
                    wgv = wg_bf[:].rearrange("p (f e) -> p f e", e=NE)
                    wbv = wb_bf[:].rearrange("p (f n) -> p f n", n=NST)
                    wcv = wc_bf[:].rearrange("p (f n) -> p f n", n=NST)
                    for fc in range(8):
                        rhs = xn_lo[:, fc, tc * 512:(tc + 1) * 512] if fc < 4 else xup[par][:, fc - 4, :]
                        nc.tensor.matmul(psZ[0:8, :], wgv[:, fc, :], rhs,
                                         start=(fc == 0), stop=(fc == 7))
                        nc.tensor.matmul(psBC[:, 0:512], wbv[:, fc, :], rhs,
                                         start=(fc == 0), stop=(fc == 7))
                        instr = nc.tensor.matmul(psBC[:, 512:1024], wcv[:, fc, :], rhs,
                                                 start=(fc == 0), stop=(fc == 7))
                    s_pe.inc(instr)

                pe_done.append(s_pe.v)
                mm_pe = s_pe.v

                @blk.vector
                def _(v, tc=tc):
                    v.wait_ge(s_pe.h, mm_pe)
                    if tc >= 1:
                        v.wait_ge(s_st.h, s_st.v)
                    nc.vector.tensor_scalar(zstage[:], psZ[0:8, :], bgv[:], None, op0=OP.add)
                    nc.vector.tensor_scalar(bstage[:], psBC[:, 0:512], bbv[:], None, op0=OP.add)
                    s_dv.inc(nc.vector.tensor_scalar(cstage2[:], psBC[:, 512:1024], bcv[:], None, op0=OP.add))

                ev_dv = s_dv.v

                @blk.sync
                def _(q, tc=tc):
                    q.wait_ge(s_dv.h, ev_dv)
                    s_st.dma(q.dma_start(out=bT_d[:, tc * 512:(tc + 1) * 512], in_=bstage[:]))
                    s_st.dma(q.dma_start(out=cT_d[:, tc * 512:(tc + 1) * 512], in_=cstage2[:]))

                if "3" in bparts:
                    @blk.vector
                    def _(v, tc=tc):
                        for g in range(4):
                            s_dv.inc(nc.vector.scalar_tensor_tensor(
                                oa[:, g, tc * 512:(tc + 1) * 512],
                                xn_lo[:, g, tc * 512:(tc + 1) * 512], dskip[:, g:g + 1],
                                oa[:, g, tc * 512:(tc + 1) * 512],
                                op0=OP.mult, op1=OP.add))

                if "2" not in bparts:
                    pe_done.append(s_pe.v)
                    pe_done.append(s_pe.v)
                    continue

                @blk.tensor
                def _(t, tc=tc):
                    t.wait_ge(s_dv.h, ev_dv)
                    for j in range(4):
                        instr = nc.tensor.transpose(psZT[:, j * 8:(j + 1) * 8],
                                                    zstage[0:8, j * 128:(j + 1) * 128],
                                                    identf[0:8, 0:8])
                    s_pe.inc(instr)

                pe_done.append(s_pe.v)
                zt_pe = s_pe.v

                @blk.vector
                def _(v, tc=tc):
                    v.wait_ge(s_pe.h, zt_pe)
                    s_dv.inc(nc.vector.tensor_copy(zt_sb[:], psZT[:].rearrange("p (j e) -> p j e", e=8)))
                    v.wait_ge(s_dv.h, s_dv.v)
                    for j in range(4):
                        s_dv.inc(nc.vector.tensor_reduce(
                            gtmp[:, j, 0:1], zt_sb[:, j, :], axis=AX.X, op=OP.max, negate=True))

                nmax_dv = s_dv.v

                @blk.vector
                def _(v, tc=tc):
                    v.wait_ge(s_dv.h, nmax_dv)
                    for j in range(4):
                        s_dv.inc(nc.vector.tensor_scalar(
                            egbuf[:, j, :], zt_sb[:, j, :], gtmp[:, j, 0:1], None,
                            op0=OP.add))

                zms_dv = s_dv.v

                @blk.scalar
                def _(a, tc=tc, zms_dv=zms_dv):
                    a.wait_ge(s_dv.h, zms_dv)
                    for j in range(4):
                        s_ac.inc(nc.scalar.activation(
                            egbuf[:, j, :], egbuf[:, j, :], AF.Exp,
                            bias=0.0, scale=1.0, accum_out=gtmp[:, j, 1:2]))

                eg_ac = s_ac.v

                @blk.vector
                def _(v, tc=tc):
                    v.wait_ge(s_ac.h, eg_ac)
                    for j in range(4):
                        nc.vector.reciprocal(gtmp[:, j, 2:3], gtmp[:, j, 1:2])
                        s_dv.inc(nc.vector.tensor_scalar(
                            gtmp[:, j, 8:16], zt_sb[:, j, :], gtmp[:, j, 0:1], 0.0,
                            op0=OP.add, op1=OP.is_ge))
                        v.wait_ge(s_dv.h, s_dv.v)
                        s_dv.inc(nc.vector.scalar_tensor_tensor(
                            gtmp[:, j, 8:16], gtmp[:, j, 8:16], -1e30, zt_sb[:, j, :],
                            op0=OP.mult, op1=OP.add))
                        v.wait_ge(s_dv.h, s_dv.v)
                        s_dv.inc(nc.vector.tensor_reduce(
                            gtmp[:, j, 3:4], gtmp[:, j, 8:16], axis=AX.X, op=OP.max, negate=True))
                        v.wait_ge(s_dv.h, s_dv.v)
                        s_dv.inc(nc.vector.tensor_scalar(
                            gtmp[:, j, 8:16], zt_sb[:, j, :], gtmp[:, j, 3:4], 0.0,
                            op0=OP.add, op1=OP.is_ge))
                        v.wait_ge(s_dv.h, s_dv.v)
                        s_dv.inc(nc.vector.scalar_tensor_tensor(
                            combT[:, j, :], egbuf[:, j, :], gtmp[:, j, 2:3], gtmp[:, j, 8:16],
                            op0=OP.mult, op1=OP.mult))

                comb_dv = s_dv.v

                @blk.tensor
                def _(t, tc=tc):
                    t.wait_ge(s_dv.h, comb_dv)
                    for j in range(4):
                        instr = nc.tensor.transpose(psCT[:, j * 128:(j + 1) * 128],
                                                    combT[:, j, :], identf[:])
                    s_pe.inc(instr)

                pe_done.append(s_pe.v)
                ct_pe = s_pe.v

                @blk.vector
                def _(v, tc=tc):
                    v.wait_ge(s_pe.h, ct_pe)
                    s_dv.inc(nc.vector.tensor_copy(cstage[:], psCT[0:8, :]))

                cs_dv = s_dv.v

                @blk.sync
                def _(q, tc=tc):
                    q.wait_ge(s_dv.h, cs_dv)
                    s_st.dma(q.dma_start(out=comb_d[:, tc * 512:(tc + 1) * 512], in_=cstage[:]))


    try:
        _psB.close()
    except NameError:
        pass
    comb_done = s_st.v

    if "C" in phases:
        with nc.Block() as blk:
            xup = [kview(0, (128, 4, 512), BF16), kview(4096, (128, 4, 512), BF16)]
            xebuf = [vview(0, (128, 8, 512), BF16), vview(8192, (128, 8, 512), BF16)]
            cbc = [vview(16384, (128, 512), BF16), vview(17408, (128, 512), BF16)]
            _psC = contextlib.ExitStack()
            psEx = _psC.enter_context(nc.psum_tensor("ps_auto7", [128, 2048], F32))[:]
            mm_done = []
            ev_done = []
            bias_done = []

            for tc in range(NTC):
                par = tc % 2

                @blk.gpsimd
                def _(g, tc=tc, par=par):
                    if tc == 0:
                        g.wait_ge(s_st.h, comb_done)
                    if tc >= 2:
                        g.wait_ge(s_pe.h, mm_done[NE * (tc - 2) + NE - 1])
                        g.wait_ge(s_pe.h, bias_done[tc - 2])
                    s_gld.dma(g.dma_start(out=xup[par], in_=xnup_dv[:, :, tc * 512:(tc + 1) * 512]))
                    s_gld.dma(g.dma_start(out=cbrow[par][:], in_=comb_d[:, tc * 512:(tc + 1) * 512]))

                for e in range(NE):
                    epar = e % 2
                    ge = tc * NE + e

                    @blk.gpsimd
                    def _(g, tc=tc, e=e, epar=epar, ge=ge):
                        if ge >= 2:
                            g.wait_ge(s_pe.h, mm_done[ge - 2])
                        s_gld.dma(g.dma_start(out=wexp_bf[epar][:], in_=wexp_d[e]))
                        src = comb_d[e:e + 1, tc * 512:(tc + 1) * 512].partition_broadcast(128)
                        s_gld.dma(g.dma_start(out=cbc[epar][:], in_=src))

                    eld = s_gld.v

                    @blk.vector
                    def _(v, tc=tc, e=e, epar=epar, par=par, ge=ge):
                        v.wait_ge(s_gld.h, eld)
                        if ge >= 2:
                            v.wait_ge(s_pe.h, mm_done[ge - 2])
                        xe = xebuf[epar]
                        nc.vector.tensor_tensor(
                            xe[:, 0:4, :],
                            xn_lo[:, :, tc * 512:(tc + 1) * 512],
                            cbc[epar][:].unsqueeze(1).broadcast_to([128, 4, 512]),
                            op=OP.mult)
                        s_dv.inc(nc.vector.tensor_tensor(
                            xe[:, 4:8, :], xup[par],
                            cbc[epar][:].unsqueeze(1).broadcast_to([128, 4, 512]),
                            op=OP.mult))

                    xe_dv = s_dv.v

                    @blk.tensor
                    def _(t, tc=tc, e=e, epar=epar):
                        t.wait_ge(s_dv.h, xe_dv)
                        if e == 0 and tc > 0:
                            t.wait_ge(s_dv.h, ev_done[tc - 1])
                        wx = wexp_bf[epar][:].rearrange("p (f g q) -> p f g q", g=4, q=128)
                        for g in range(4):
                            for fc in range(8):
                                instr = nc.tensor.matmul(
                                    psEx[:, g * 512:(g + 1) * 512],
                                    wx[:, fc, g, :], xebuf[epar][:, fc, :],
                                    start=(e == 0 and fc == 0), stop=False)
                        s_pe.inc(instr)

                    mm_done.append(s_pe.v)

                @blk.tensor
                def _(t, tc=tc, par=par):
                    for g in range(4):
                        instr = nc.tensor.matmul(
                            psEx[:, g * 512:(g + 1) * 512],
                            bexp_bf[:, g * 128:(g + 1) * 128], cbrow[par][:],
                            start=False, stop=True)
                    s_pe.inc(instr)

                bias_done.append(s_pe.v)
                ex_pe = s_pe.v

                @blk.vector
                def _(v, tc=tc, ex_pe=ex_pe):
                    v.wait_ge(s_pe.h, ex_pe)
                    for g in range(4):
                        s_dv.inc(nc.vector.scalar_tensor_tensor(
                            oa[:, g, tc * 512:(tc + 1) * 512],
                            psEx[:, g * 512:(g + 1) * 512], 1.0,
                            oa[:, g, tc * 512:(tc + 1) * 512],
                            op0=OP.mult, op1=OP.add))

                ev_done.append(s_dv.v)

    try:
        _psC.close()
    except NameError:
        pass

    if "D" in phases:
        with nc.Block() as blk:
            TB = 512
            NTB = S // TB
            dt_sb = kview(0, (128, TB), F32)
            e1_sb = kview(2048, (128, TB), F32)
            w_sb = kview(4096, (128, TB), F32)
            a_sb = [kview(6144, (128, TB), F32), kview(8192, (128, TB), F32)]
            u_sb = [kview(10240, (128, TB), F32), kview(12288, (128, TB), F32)]
            h_sb = [kview(14336, (128, TB), F32), kview(16384, (128, TB), F32)]
            hc_sb = [kview(18432, (128, TB), BF16), kview(19456, (128, TB), BF16)]
            brep = [kview(20480, (128, TB), BF16), kview(21504, (128, TB), BF16)]
            crep = [kview(22528, (128, TB), BF16), kview(23552, (128, TB), BF16)]
            xupd = [kview(24576, (128, 4, TB), BF16), kview(28672, (128, 4, TB), BF16)]
            @blk.gpsimd
            def _(g):
                s_gld.dma(g.dma_start(out=wbig[:], in_=wd_d))

            wdld = s_gld.v
            _psD = contextlib.ExitStack()
            psDt = _psD.enter_context(nc.psum_tensor("ps_auto8", [128, 512], F32))[:]
            psR = [_psD.enter_context(nc.psum_tensor(f"ps_psR{i}", [128, 1024], F32))[:] for i in range(2)]
            psY = _psD.enter_context(nc.psum_tensor("ps_auto10", [128, 512], F32))[:]
            it_end_dv = []
            it_end_po = []
            k_exp_ac = {}
            k_u_dv = {}
            k_hc_po = {}

            for g4 in range(4):
                for tb in range(NTB):
                    it = g4 * NTB + tb
                    par = it % 2

                    @blk.gpsimd
                    def _(gq, g4=g4, tb=tb, par=par, it=it):
                        if it == 0:
                            gq.wait_ge(s_st.h, comb_done)
                        if it >= 2:
                            gq.wait_ge(s_dv.h, it_end_dv[it - 2])
                            gq.wait_ge(s_po.h, it_end_po[it - 2])
                        s_gld.dma(gq.dma_start(out=xupd[par],
                                               in_=xnup_dv[:, :, tb * TB:(tb + 1) * TB]))
                        bsrc = bT_d[:, tb * TB:(tb + 1) * TB].unsqueeze(0).broadcast_to([8, NST, TB])
                        csrc = cT_d[:, tb * TB:(tb + 1) * TB].unsqueeze(0).broadcast_to([8, NST, TB])
                        s_gld.dma(gq.dma_start(out=brep[par][:], in_=bsrc))
                        s_gld.dma(gq.dma_start(out=crep[par][:], in_=csrc))

                    ld_here = s_gld.v

                    @blk.tensor
                    def _(t, g4=g4, tb=tb, par=par, it=it):
                        t.wait_ge(s_gld.h, ld_here)
                        t.wait_ge(s_ac.h, s_ac.v)
                        if it == 0:
                            t.wait_ge(s_gld.h, wdld)
                        wdv = wbig[:].rearrange("p (f c) -> p f c", c=DH)
                        for fc in range(8):
                            rhs = (xn_lo[:, fc, tb * TB:(tb + 1) * TB] if fc < 4
                                   else xupd[par][:, fc - 4, :])
                            instr = nc.tensor.matmul(
                                psDt[:], wdv[:, fc, g4 * 128:(g4 + 1) * 128], rhs,
                                start=(fc == 0), stop=(fc == 7))
                        s_pe.inc(instr)

                    dt_pe = s_pe.v

                    @blk.scalar
                    def _(a, g4=g4):
                        a.wait_ge(s_pe.h, dt_pe)
                        a.wait_ge(s_dv.h, s_dv.v)
                        s_ac.inc(nc.scalar.activation(e1_sb[:], psDt[:], AF.Exp,
                                                      bias=bd[:, g4:g4 + 1], scale=1.0))
                        a.wait_ge(s_ac.h, s_ac.v)
                        s_ac.inc(nc.scalar.activation(dt_sb[:], e1_sb[:], AF.Ln, bias=1.0, scale=1.0))

                    sp_ac = s_ac.v

                    @blk.vector
                    def _(v, g4=g4, tb=tb):
                        v.wait_ge(s_ac.h, sp_ac)
                        s_dv.inc(nc.vector.tensor_tensor(
                            w_sb[:], dt_sb[:], xn_lo[:, g4, tb * TB:(tb + 1) * TB], op=OP.mult))

                    w_dv = s_dv.v

                    for k16 in range(16):
                        kpar = k16 % 2
                        kglob = g4 * 16 + k16
                        r = k16 // 4
                        m = k16 % 4
                        kk = (it, k16)

                        @blk.tensor
                        def _(t, r=r, m=m, k16=k16, kpar=kpar, it=it):
                            t.wait_ge(s_dv.h, w_dv)
                            prev = (it, k16 - 2) if k16 >= 2 else ((it - 1, 14 + k16) if it >= 1 else None)
                            if prev is not None and prev in k_exp_ac:
                                t.wait_ge(s_ac.h, k_exp_ac[prev])
                                t.wait_ge(s_dv.h, k_u_dv[prev])
                            lhs = r1[32 * r:32 * (r + 1), m * 128:(m + 1) * 128]
                            nc.tensor.matmul(psR[kpar][:, 0:512], lhs,
                                             dt_sb[32 * r:32 * (r + 1), :],
                                             start=True, stop=True, tile_position=(32 * r, 0))
                            instr = nc.tensor.matmul(psR[kpar][:, 512:1024], lhs,
                                                     w_sb[32 * r:32 * (r + 1), :],
                                                     start=True, stop=True, tile_position=(32 * r, 0))
                            s_pe.inc(instr)

                        rep_pe = s_pe.v

                        @blk.scalar
                        def _(a, kglob=kglob, kpar=kpar, kk=kk):
                            a.wait_ge(s_pe.h, rep_pe)
                            k_exp_ac[kk] = s_ac.inc(nc.scalar.activation(
                                a_sb[kpar][:], psR[kpar][:, 0:512], AF.Exp,
                                bias=0.0, scale=a_il[:, kglob:kglob + 1]))

                        @blk.vector
                        def _(v, kpar=kpar, par=par, tb=tb, kglob=kglob, kk=kk):
                            v.wait_ge(s_pe.h, rep_pe)
                            prevhc = (kk[0], kk[1] - 2) if kk[1] >= 2 else ((kk[0] - 1, 14 + kk[1]) if kk[0] >= 1 else None)
                            if prevhc is not None and prevhc in k_hc_po:
                                v.wait_ge(s_po.h, k_hc_po[prevhc])
                            k_u_dv[kk] = s_dv.inc(nc.vector.tensor_tensor(
                                u_sb[kpar][:], psR[kpar][:, 512:1024], brep[par][:], op=OP.mult))
                            v.wait_ge(s_ac.h, k_exp_ac[kk])
                            v.wait_ge(s_dv.h, s_dv.v)
                            init = 0.0 if tb == 0 else state[:, kglob:kglob + 1]
                            s_dv.inc(nc.vector.tensor_tensor_scan(
                                h_sb[kpar][:], a_sb[kpar][:], u_sb[kpar][:], init,
                                op0=OP.mult, op1=OP.add))
                            v.wait_ge(s_dv.h, s_dv.v)
                            s_dv.inc(nc.vector.tensor_copy(state[:, kglob:kglob + 1],
                                                           h_sb[kpar][:, TB - 1:TB]))

                        h_dv = s_dv.v

                        @blk.gpsimd
                        def _(gq, kpar=kpar, par=par, kk=kk):
                            gq.wait_ge(s_dv.h, h_dv)
                            k_hc_po[kk] = s_po.inc(nc.gpsimd.tensor_tensor(
                                hc_sb[kpar][:], h_sb[kpar][:], crep[par][:], op=OP.mult))

                        hc_po = s_po.v

                        @blk.tensor
                        def _(t, k16=k16, kpar=kpar, it=it):
                            t.wait_ge(s_po.h, hc_po)
                            if k16 == 0 and it > 0:
                                t.wait_ge(s_dv.h, it_end_dv[it - 1])
                            instr = nc.tensor.matmul(
                                psY[:], zred[:, k16 * 128:(k16 + 1) * 128], hc_sb[kpar][:],
                                start=(k16 == 0), stop=(k16 == 15))
                            s_pe.inc(instr)

                    y_pe = s_pe.v

                    @blk.vector
                    def _(v, g4=g4, tb=tb):
                        v.wait_ge(s_pe.h, y_pe)
                        s_dv.inc(nc.vector.scalar_tensor_tensor(
                            oa[:, g4, tb * TB:(tb + 1) * TB], psY[:], 1.0,
                            oa[:, g4, tb * TB:(tb + 1) * TB], op0=OP.mult, op1=OP.add))

                    it_end_dv.append(s_dv.v)
                    it_end_po.append(s_po.v)

    try:
        _psD.close()
    except NameError:
        pass

    if "E" in phases:
        with nc.Block() as blk:
            xup = [_mkview(wexp_bf[0], 0, (128, 4, 512), BF16),
                   _mkview(wexp_bf[0], 4096, (128, 4, 512), BF16)]
            vstage = _mkview(wexp_bf[1], 0, (128, 4, 512), BF16)
            qstage = _mkview(wexp_bf[1], 4096, (128, 4, 512), BF16)
            _psE = contextlib.ExitStack()
            psP = _psE.enter_context(nc.psum_tensor("ps_autoP", [128, 2048], F32))[:]
            psVT = [_psE.enter_context(nc.psum_tensor(f"ps_psVT{i}", [128, 512], BF16))[:] for i in range(2)]

            @blk.gpsimd
            def _(g):
                s_po.inc(nc.gpsimd.memset(vt_h[:, :, :, 64:65], 1.0))

            ones_po = s_po.v
            proj_pe = []

            for di, dst in enumerate(("k", "v", "q")):
                w_src = {"k": wk_d, "v": wv_d, "q": wq_d}[dst]

                @blk.gpsimd
                def _(g, di=di, dst=dst, w_src=w_src):
                    if di > 0:
                        g.wait_ge(s_pe.h, proj_pe[-1])
                    s_gld.dma(g.dma_start(out=wbig[:], in_=w_src))

                wld = s_gld.v

                for tc in range(NTC):
                    par = tc % 2

                    @blk.gpsimd
                    def _(g, tc=tc, par=par):
                        if len(proj_pe) >= 2:
                            g.wait_ge(s_pe.h, proj_pe[-2])
                        s_gld.dma(g.dma_start(out=xup[par],
                                              in_=xnup_dv[:, :, tc * 512:(tc + 1) * 512]))

                    ld_here = s_gld.v

                    @blk.tensor
                    def _(t, tc=tc, par=par, wld=wld, ld_here=ld_here):
                        t.wait_ge(s_gld.h, max(wld, ld_here))
                        t.wait_ge(s_dv.h, s_dv.v)
                        w = wbig[:].rearrange("p (f c) -> p f c", c=DH)
                        for pg in range(4):
                            for fc in range(8):
                                rhs = (xn_lo[:, fc, tc * 512:(tc + 1) * 512] if fc < 4
                                       else xup[par][:, fc - 4, :])
                                instr = nc.tensor.matmul(
                                    psP[:, pg * 512:(pg + 1) * 512],
                                    w[:, fc, pg * 128:(pg + 1) * 128], rhs,
                                    start=(fc == 0), stop=(fc == 7))
                        s_pe.inc(instr)

                    proj_pe.append(s_pe.v)
                    mm_pe = s_pe.v

                    if dst == "k":
                        @blk.vector
                        def _(v, tc=tc, mm_pe=mm_pe):
                            v.wait_ge(s_pe.h, mm_pe)
                            for pg in range(4):
                                s_dv.inc(nc.vector.tensor_scalar(
                                    KT_v[:, pg, tc * 512:(tc + 1) * 512],
                                    psP[:, pg * 512:(pg + 1) * 512],
                                    bk[:, pg:pg + 1], None, op0=OP.add))
                    elif dst == "v":
                        @blk.vector
                        def _(v, tc=tc, mm_pe=mm_pe):
                            v.wait_ge(s_pe.h, mm_pe)
                            for pg in range(4):
                                s_dv.inc(nc.vector.tensor_scalar(
                                    vstage[:, pg, :], psP[:, pg * 512:(pg + 1) * 512],
                                    bv[:, pg:pg + 1], None, op0=OP.add))

                        vs_dv = s_dv.v

                        for pg in range(4):
                            @blk.tensor
                            def _(t, tc=tc, pg=pg, vs_dv=vs_dv):
                                t.wait_ge(s_dv.h, s_dv.v)
                                if tc == 0 and pg == 0:
                                    t.wait_ge(s_po.h, ones_po)
                                for j in range(4):
                                    instr = nc.tensor.transpose(
                                        psVT[pg % 2][:, j * 128:(j + 1) * 128],
                                        vstage[:, pg, :][:, j * 128:(j + 1) * 128], identb[:])
                                s_pe.inc(instr)

                            vt_pe = s_pe.v

                            @blk.vector
                            def _(v, tc=tc, pg=pg, vt_pe=vt_pe):
                                v.wait_ge(s_pe.h, vt_pe)
                                s_dv.inc(nc.vector.tensor_copy(
                                    vt_h[:, tc * 4:(tc + 1) * 4, 2 * pg:2 * pg + 2, 0:64],
                                    psVT[pg % 2][:].rearrange("p (a h j) -> p a h j", h=2, j=64)))
                    else:
                        @blk.vector
                        def _(v, tc=tc, mm_pe=mm_pe):
                            v.wait_ge(s_pe.h, mm_pe)
                            if tc >= 1:
                                v.wait_ge(s_st.h, s_st.v)
                            for pg in range(4):
                                s_dv.inc(nc.vector.tensor_scalar(
                                    qstage[:, pg, :], psP[:, pg * 512:(pg + 1) * 512],
                                    bq[:, pg:pg + 1], None, op0=OP.add))

                        q_dv = s_dv.v

                        @blk.sync
                        def _(q, tc=tc, q_dv=q_dv):
                            q.wait_ge(s_dv.h, q_dv)
                            s_st.dma(q.dma_start(
                                out=qt_dv[:, :, tc * 512:(tc + 1) * 512], in_=qstage[:]))

    try:
        _psE.close()
    except NameError:
        pass
    qt_done = s_st.v

    if "F" in phases:
        with nc.Block() as blk:
            NKT = S // 128
            NQC = S // 512
            pT = [[xview(0, (128, 1024), BF16), xview(2048, (128, 1024), BF16)],
                  [xview(4096, (128, 1024), BF16), xview(6144, (128, 1024), BF16)]]
            attnU = xview(8192, (128, NQC * 512), BF16)
            tmpn = xview(8192 + NQC * 1024, (128, 512), F32)
            stage64 = xview(8192 + NQC * 1024 + 2048, (128, 1024), F32)
            dtile = xview(8192 + NQC * 1024 + 6144, (16, 512), F32)
            rdtile = xview(8192 + NQC * 1024 + 8192, (16, 512), F32)
            tmpb = xview(8192 + NQC * 1024 + 10240, (128, 512), BF16)
            _psF = contextlib.ExitStack()
            psS = [_psF.enter_context(nc.psum_tensor(f"ps_psS{i}", [128, 1024], F32))[:] for i in range(2)]
            psO = [_psF.enter_context(nc.psum_tensor(f"ps_psO{i}", [128, 512], F32))[:] for i in range(2)]
            psD = _psF.enter_context(nc.psum_tensor("ps_auto15", [128, 512], F32))[:]
            exp_ac = {}
            pv_pe = {}
            nrm_dv = [0]
            nrm_pe = {}

            for pr in range(4):
                @blk.sync
                def _(q, pr=pr):
                    if pr == 0:
                        q.wait_ge(s_st.h, qt_done)
                    if pr >= 1:
                        q.wait_ge(s_dv.h, nrm_dv[-1])
                    s_st.dma(q.dma_start(out=qt_pair[:], in_=qt_dv[:, pr, :]))

                qld = s_st.v

                den_dma = []
                evac_dv = []
                for qc in range(NQC):
                    for g2 in range(NKT // 2):
                        kk = (pr, qc, g2)

                        for hh in range(2):
                            @blk.tensor
                            def _(t, pr=pr, qc=qc, g2=g2, hh=hh, kk=kk):
                                if qc == 0 and g2 == 0 and hh == 0:
                                    t.wait_ge(s_st.h, qld)
                                    t.wait_ge(s_dv.h, s_dv.v)
                                prev = (pr, qc, g2 - 1) if g2 >= 1 else \
                                    ((pr, qc - 1, NKT // 2 - 1) if qc >= 1 else
                                     ((pr - 1, NQC - 1, NKT // 2 - 1) if pr >= 1 else None))
                                if prev is not None and (prev, hh) in exp_ac:
                                    t.wait_ge(s_ac.h, exp_ac[(prev, hh)])
                                for kj in range(2):
                                    kt = g2 * 2 + kj
                                    instr = nc.tensor.matmul(
                                        psS[hh][:, kj * 512:(kj + 1) * 512],
                                        KT_v[64 * hh:64 * (hh + 1), pr, kt * 128:(kt + 1) * 128],
                                        qt_pair[64 * hh:64 * (hh + 1), qc * 512:(qc + 1) * 512],
                                        start=True, stop=True, tile_position=(64 * hh, 0))
                                s_pe.inc(instr)

                            sc_pe = s_pe.v

                            @blk.scalar
                            def _(a, hh=hh, g2=g2, kk=kk, sc_pe=sc_pe):
                                a.wait_ge(s_pe.h, sc_pe)
                                prev2 = (kk[0], kk[1], kk[2] - 2)
                                if (prev2, hh) in pv_pe:
                                    a.wait_ge(s_pe.h, pv_pe[(prev2, hh)])
                                exp_ac[(kk, hh)] = s_ac.inc(nc.scalar.activation(
                                    pT[hh][g2 % 2][:], psS[hh][:], AF.Exp,
                                    bias=-4.0, scale=0.125))

                        for hh in range(2):
                            @blk.tensor
                            def _(t, pr=pr, qc=qc, g2=g2, hh=hh, kk=kk):
                                t.wait_ge(s_ac.h, exp_ac[(kk, hh)])
                                if g2 == 0 and qc >= 1:
                                    t.wait_ge(s_dv.h, evac_dv[-1])
                                h_core = 2 * pr + hh
                                for kj in range(2):
                                    kt = g2 * 2 + kj
                                    instr = nc.tensor.matmul(
                                        psO[hh][0:65, :],
                                        vt_v[:, kt, 65 * h_core:65 * h_core + 65],
                                        pT[hh][g2 % 2][:, kj * 512:(kj + 1) * 512],
                                        start=(g2 == 0 and kj == 0),
                                        stop=(g2 == NKT // 2 - 1 and kj == 1))
                                pv_pe[(kk, hh)] = s_pe.inc(instr)

                    o_pe = s_pe.v

                    @blk.vector
                    def _(v, pr=pr, qc=qc, o_pe=o_pe):
                        v.wait_ge(s_pe.h, o_pe)
                        if qc >= 1:
                            v.wait_ge(s_st.h, den_dma[-1])
                        nc.vector.tensor_copy(attnU[0:64, qc * 512:(qc + 1) * 512], psO[0][0:64, :])
                        nc.vector.tensor_copy(tmpb[0:64, :], psO[1][0:64, :])
                        nc.vector.tensor_copy(stage64[64:65, 0:512], psO[0][64:65, :])
                        s_dv.inc(nc.vector.tensor_copy(stage64[64:65, 512:1024], psO[1][64:65, :]))

                    st_dv = s_dv.v

                    @blk.tensor
                    def _(t, qc=qc, st_dv=st_dv):
                        t.wait_ge(s_dv.h, st_dv)
                        s_pe.inc(nc.tensor.matmul(
                            psD[64:128, :], identb[0:64, 0:64], tmpb[0:64, :],
                            start=True, stop=True, tile_position=(0, 64)))

                    sh_pe = s_pe.v

                    @blk.vector
                    def _(v, qc=qc, sh_pe=sh_pe):
                        v.wait_ge(s_pe.h, sh_pe)
                        s_dv.inc(nc.vector.tensor_copy(
                            attnU[64:128, qc * 512:(qc + 1) * 512], psD[64:128, :]))

                    evac_dv.append(s_dv.v)

                    @blk.sync
                    def _(q, pr=pr, qc=qc, st_dv=st_dv):
                        q.wait_ge(s_dv.h, st_dv)
                        s_st.dma(q.dma_start(
                            out=den_d[2 * pr * 8 + qc:2 * pr * 8 + qc + 1, :],
                            in_=stage64[64:65, 0:512]))
                        s_st.dma(q.dma_start(
                            out=den_d[(2 * pr + 1) * 8 + qc:(2 * pr + 1) * 8 + qc + 1, :],
                            in_=stage64[64:65, 512:1024]))

                    den_dma.append(s_st.v)

                @blk.sync
                def _(q, pr=pr):
                    q.wait_ge(s_st.h, den_dma[-1])
                    s_ld.dma(q.dma_start(out=dtile[:], in_=den_d[pr * 16:(pr + 1) * 16, :]))

                dt_ld = s_ld.v

                @blk.vector
                def _(v, pr=pr, dt_ld=dt_ld):
                    v.wait_ge(s_ld.h, dt_ld)
                    s_dv.inc(nc.vector.reciprocal(rdtile[:], dtile[:]))

                rd_dv = s_dv.v

                @blk.sync
                def _(q, pr=pr, rd_dv=rd_dv):
                    q.wait_ge(s_dv.h, rd_dv)
                    s_st.dma(q.dma_start(out=rden_d[pr * 16:(pr + 1) * 16, :], in_=rdtile[:]))

                rd_st = s_st.v

                for qc in range(NQC):
                    sl0 = 0
                    sl1 = 1

                    @blk.gpsimd
                    def _(g, pr=pr, qc=qc, sl0=sl0, sl1=sl1, rd_st=rd_st):
                        g.wait_ge(s_st.h, rd_st)
                        if qc >= 1 and (pr, qc - 1) in nrm_pe:
                            g.wait_ge(s_pe.h, nrm_pe[(pr, qc - 1)])
                        s_gld.dma(g.dma_start(
                            out=srow[0:1, sl0 * 512:(sl0 + 1) * 512],
                            in_=rden_d[2 * pr * 8 + qc:2 * pr * 8 + qc + 1, :]))
                        s_gld.dma(g.dma_start(
                            out=srow[0:1, sl1 * 512:(sl1 + 1) * 512],
                            in_=rden_d[(2 * pr + 1) * 8 + qc:(2 * pr + 1) * 8 + qc + 1, :]))

                    sr_gld = s_gld.v

                    @blk.tensor
                    def _(t, qc=qc, sl0=sl0, sl1=sl1, sr_gld=sr_gld):
                        t.wait_ge(s_gld.h, sr_gld)
                        t.wait_ge(s_dv.h, s_dv.v)
                        nc.tensor.matmul(
                            psD[0:64, :], onesb[0:1, 0:64],
                            srow[0:1, sl0 * 512:(sl0 + 1) * 512],
                            start=True, stop=True, tile_position=(0, 0))
                        instr = nc.tensor.matmul(
                            psD[64:128, :], onesb[0:1, 0:64],
                            srow[0:1, sl1 * 512:(sl1 + 1) * 512],
                            start=True, stop=True, tile_position=(0, 64))
                        nrm_pe[(pr, qc)] = s_pe.inc(instr)

                    nr_pe = s_pe.v

                    @blk.vector
                    def _(v, pr=pr, qc=qc, nr_pe=nr_pe):
                        v.wait_ge(s_pe.h, nr_pe)
                        s_dv.inc(nc.vector.tensor_tensor(
                            tmpn[:], attnU[:, qc * 512:(qc + 1) * 512], psD[:], op=OP.mult))
                        v.wait_ge(s_dv.h, s_dv.v)
                        s_dv.inc(nc.vector.tensor_tensor(
                            oa[:, pr, qc * 512:(qc + 1) * 512],
                            oa[:, pr, qc * 512:(qc + 1) * 512], tmpn[:], op=OP.add))

                nrm_dv.append(s_dv.v)

    try:
        _psF.close()
    except NameError:
        pass

    with nc.Block() as blk:
        @blk.sync
        def _(q):
            q.wait_ge(s_dv.h, s_dv.v)
            q.wait_ge(s_pe.h, s_pe.v)
            q.wait_ge(s_ac.h, s_ac.v)
            q.wait_ge(s_po.h, s_po.v)
            for g in range(4):
                s_st.dma(q.dma_start(out=out_d[:, g * S:(g + 1) * S],
                                     in_=out_acc[:, g * S:(g + 1) * S]))
            if debug:
                for nm, src, shape, dt in [
                    ("xnup", xnup_d, (128, 4 * S), BF16),
                    ("comb", comb_d, (NE, S), F32),
                    ("bT", bT_d, (NST, S), F32),
                    ("cT", cT_d, (NST, S), F32),
                    ("qt", qt_d, (128, 4 * S), BF16),
                    ("den", den_d, (64, 512), F32),
                ]:
                    dap = nc.dram_tensor("dbg_" + nm, list(shape), dt,
                                         kind="ExternalOutput").ap()
                    s_st.dma(q.dma_start(out=dap, in_=src))
            q.wait_ge(s_st.h, s_st.v)

    return nc


def _prep_core_inputs(inputs, c, S):
    b, h = c // 2, c % 2
    own = slice(h * DH, (h + 1) * DH)
    perm = np.r_[h * DH:(h + 1) * DH, (1 - h) * DH:(2 - h) * DH]

    x = np.ascontiguousarray(inputs["L0_attn_x"][b, :S, :][:, perm], np.float32)

    Wd = inputs["L0_attn_Wd"][perm][:, own]
    Wb = inputs["L0_attn_Wb"][perm]
    Wc = inputs["L0_attn_Wc"][perm]
    Wg = inputs["L0_ffn_W_gate"][perm]
    Wqkv = inputs["L0_ffn_W_qkv"]
    Wq = Wqkv[perm][:, 0 * D:1 * D][:, own]
    Wk = Wqkv[perm][:, 1 * D:2 * D][:, own]
    Wv = Wqkv[perm][:, 2 * D:3 * D][:, own]
    bqkv = inputs["L0_ffn_b_qkv"]
    Wexp = inputs["L0_ffn_W_experts"][:, perm][:, :, own]
    bexp = inputs["L0_ffn_b_experts"][:, own]

    col4 = lambda v: np.ascontiguousarray(np.asarray(v, np.float32).reshape(4, 128).T)
    f8 = lambda w, n: np.ascontiguousarray(
        np.asarray(w, np.float32).reshape(8, 128, n).transpose(1, 0, 2))

    A_own = np.asarray(inputs["L0_attn_A_log"][own], np.float32)
    alog = np.ascontiguousarray(
        A_own.reshape(64, 8, NST).transpose(1, 2, 0).reshape(128, 64))

    p_idx = np.arange(128)
    r1 = np.zeros((128, 512), np.float32)
    for m in range(4):
        for q in range(128):
            r1[np.where(p_idx % 32 == 8 * m + q // NST)[0], m * 128 + q] = 1.0
    zr = np.zeros((128, 2048), np.float32)
    for k16 in range(16):
        for p in range(128):
            zr[p, k16 * 128 + 8 * k16 + p // NST] = 1.0

    return dict(
        x=x,
        identf=np.eye(128, dtype=np.float32),
        identb=np.eye(128, dtype=ml_dtypes.bfloat16),
        onesb=np.ones((1, 128), ml_dtypes.bfloat16),
        r1=r1,
        zred=zr.astype(ml_dtypes.bfloat16),
        alog=alog,
        dskip=col4(inputs["L0_attn_D_skip"][own]),
        wd=f8(Wd, DH), bd=col4(inputs["L0_attn_bd"][own]),
        wb=f8(Wb, NST),
        bb=np.ascontiguousarray(np.asarray(inputs["L0_attn_bb"], np.float32).reshape(NST, 1)),
        wc=f8(Wc, NST),
        bc=np.ascontiguousarray(np.asarray(inputs["L0_attn_bc"], np.float32).reshape(NST, 1)),
        wg=f8(Wg, NE),
        bg=np.ascontiguousarray(np.asarray(inputs["L0_ffn_b_gate"], np.float32).reshape(NE, 1)),
        wq=f8(Wq, DH), wk=f8(Wk, DH), wv=f8(Wv, DH),
        bq=col4(bqkv[0 * D:1 * D][own]),
        bk=col4(bqkv[1 * D:2 * D][own]),
        bv=col4(bqkv[2 * D:3 * D][own]),
        wexp=np.ascontiguousarray(
            np.asarray(Wexp, np.float32).reshape(NE, 8, 128, 4, 128).transpose(0, 2, 1, 3, 4)),
        bexp=np.ascontiguousarray(bexp, np.float32),
        cvals=np.ascontiguousarray(np.broadcast_to(
            np.asarray(CONST_VALS, np.float32), (128, 8))),
    )


def _assemble(results, S):
    out = np.empty((4, S, D), np.float32)
    for c in range(8):
        b, h = c // 2, c % 2
        shard = np.asarray(results[c]["out"])
        shard = shard.reshape(128, 4, S).transpose(1, 0, 2).reshape(DH, S)
        out[b, :, h * DH:(h + 1) * DH] = shard.T
    return out


def kernel(**inputs):
    S = inputs["L0_attn_x"].shape[1]
    nc = build_nc(S)
    in_maps = [_prep_core_inputs(inputs, c, S) for c in range(8)]
    res = run_bass_kernel_spmd(nc, in_maps, core_ids=list(range(8)))
    return _assemble(res.results, S)


# version 03
# speedup vs baseline: 2.0800x; 2.0800x over previous
import contextlib
import numpy as np
import ml_dtypes
import concourse.bass as bass
import concourse.mybir as mybir
from concourse.bass_utils import run_bass_kernel_spmd
from concourse.alu_op_type import AluOpType as OP

F32 = mybir.dt.float32
F32R = mybir.dt.float32r
BF16 = mybir.dt.bfloat16
AF = mybir.ActivationFunctionType
AX = mybir.AxisListType

D = 1024
DH = 512
NHEAD = 8
HD = 64
NE = 8
NST = 16

PHASES = "ABCDEFG"
CONST_VALS = [1e-5, -0.5, -4.0, 0.125, 1.0, 0.0, -1.0, 2.0]


class Sem:
    def __init__(self, nc, name):
        self.h = nc.alloc_semaphore(name)
        self.v = 0

    def inc(self, instr, n=1):
        instr.then_inc(self.h, n)
        self.v += n
        return self.v

    def dma(self, instr):
        return self.inc(instr, 16)


def _mkview(base, byte_off, shape, dt):
    ap = base[:, byte_off // 2:]
    ap = ap.bitcast(dt)
    n = int(np.prod(shape[1:]))
    ap = ap[0:shape[0], 0:n]
    if len(shape) == 3:
        ap = ap.rearrange("p (a b) -> p a b", b=shape[2])
    return ap


def build_nc(S=4096, phases=PHASES, debug=False, bparts="123"):
    NTC = S // 512
    NT32 = S // 128

    nc = bass.Bass("TRN2", target_bir_lowering=False)

    def din(name, shape, dt=F32):
        return nc.dram_tensor(name, list(shape), dt, kind="ExternalInput").ap()

    x_d = din("x", (S, D))
    identf_d = din("identf", (128, 128))
    identb_d = din("identb", (128, 128), BF16)
    onesb_d = din("onesb", (1, 128), BF16)
    r1f_d = din("r1f", (128, 2048), BF16)
    zred_d = din("zred", (128, 2048), BF16)
    alog_d = din("alog", (128, 64))
    dskip_d = din("dskip", (128, 4))
    wd_d = din("wd", (128, 8, DH))
    bd_d = din("bd", (128, 4))
    wb_d = din("wb", (128, 8, NST))
    bb_d = din("bb", (NST, 1))
    wc_d = din("wc", (128, 8, NST))
    bc_d = din("bc", (NST, 1))
    wg_d = din("wg", (128, 8, NE))
    bg_d = din("bg", (NE, 1))
    wq_d = din("wq", (128, 8, DH))
    wk_d = din("wk", (128, 8, DH))
    wv_d = din("wv", (128, 8, DH))
    bq_d = din("bq", (128, 4))
    bk_d = din("bk", (128, 4))
    bv_d = din("bv", (128, 4))
    wexp_d = din("wexp", (NE, 128, 8, 4, 128))
    bexp_d = din("bexp", (NE, DH))
    cvals_d = din("cvals", (128, 8))

    out_d = nc.dram_tensor("out", [128, 4 * S], F32, kind="ExternalOutput").ap()

    def dscr(name, shape, dt=F32):
        return nc.dram_tensor(name, list(shape), dt).ap()

    xnup_d = dscr("xnup_s", (128, 4 * S), BF16)
    comb_d = dscr("comb_s", (NE, S))
    bT_d = dscr("bT_s", (NST, S))
    cT_d = dscr("cT_s", (NST, S))
    qt_d = dscr("qt_s", (128, 4 * S), BF16)
    den_d = dscr("den_s", (64, 512))
    rden_d = dscr("rden_s", (64, 512))

    sb = lambda n, s, d=F32: nc.alloc_sbuf_tensor(n, list(s), d)
    out_acc = sb("out_acc", (128, 4 * S))
    xnlow = sb("xnlow", (128, max(4 * S, 16384)), BF16)
    KT = sb("KT", (128, max(4 * S, 16384)), BF16)
    vtok = sb("vtok", (128, max(NT32 * 520, 16384)), BF16)
    qt_pair = sb("qt_pair", (128, S), BF16)
    identf = sb("sb_identf", (128, 128))
    identb = sb("sb_identb", (128, 128), BF16)
    onesb = sb("sb_onesb", (1, 128), BF16)

    alog = sb("sb_alog", (128, 64))
    a_il = sb("a_il", (128, 64))
    dskip = sb("sb_dskip", (128, 4))
    bd = sb("sb_bd", (128, 4))
    bbv = sb("bbv", (NST, 1))
    bcv = sb("bcv", (NST, 1))
    bgv = sb("bgv", (NE, 1))
    bq = sb("sb_bq", (128, 4))
    bk = sb("sb_bk", (128, 4))
    bv = sb("sb_bv", (128, 4))
    state = sb("state", (128, 64))
    wbig = sb("wbig", (128, 8 * DH), BF16)
    wb_bf = sb("wb_bf", (128, 8 * NST), BF16)
    wc_bf = sb("wc_bf", (128, 8 * NST), BF16)
    wg_bf = sb("wg_bf", (128, 8 * NE), BF16)
    bexp_bf = sb("bexp_bf", (NE, DH), BF16)
    wexp_bf = [sb(f"wexp_bf{i}", (128, 4096), BF16) for i in range(2)]
    cbrow = [sb(f"cbrow{i}", (NE, 512), BF16) for i in range(2)]
    srow = sb("srow", (1, 1024), BF16)
    cvals = sb("cvals_sb", (128, 8))
    for ci, cv in enumerate(CONST_VALS):
        nc.const_aps.register(F32, cv, cvals[:, ci:ci + 1])

    kview = lambda o, s, d: _mkview(KT, o, s, d)
    xview = lambda o, s, d: _mkview(xnlow, o, s, d)
    vview = lambda o, s, d: _mkview(vtok, o, s, d)

    xn_lo = xnlow[:, 0:4 * S].rearrange("p (f t) -> p f t", t=S)
    oa = out_acc[:].rearrange("p (g t) -> p g t", t=S)
    xnup_dv = xnup_d.rearrange("p (f t) -> p f t", t=S)
    KT_v = KT[:, 0:4 * S].rearrange("p (pg t) -> p pg t", t=S)
    vt_v = vtok[:, 0:NT32 * 520].rearrange("p (tt c) -> p tt c", c=520)
    vt_h = vtok[:, 0:NT32 * 520].rearrange("p (tt h j) -> p tt h j", h=8, j=65)
    qt_dv = qt_d.rearrange("p (pg t) -> p pg t", t=S)

    s_ld = Sem(nc, "s_ld")
    s_gld = Sem(nc, "s_gld")
    s_pe = Sem(nc, "s_pe")
    s_dv = Sem(nc, "s_dv")
    s_ac = Sem(nc, "s_ac")
    s_po = Sem(nc, "s_po")
    s_st = Sem(nc, "s_st")

    with nc.Block() as blk:
        @blk.sync
        def _(q):
            for dst, src in [
                (identf[:], identf_d), (identb[:], identb_d), (onesb[:], onesb_d),
                (alog[:], alog_d),
                (dskip[:], dskip_d), (bd[:], bd_d), (bbv[:], bb_d), (bcv[:], bc_d),
                (bgv[:], bg_d), (bq[:], bq_d), (bk[:], bk_d), (bv[:], bv_d),
                (cvals[:], cvals_d),
            ]:
                s_ld.dma(q.dma_start(out=dst, in_=src))

        consts_ld = s_ld.v

        @blk.scalar
        def _(a):
            a.wait_ge(s_ld.h, consts_ld)
            s_ac.inc(nc.scalar.activation(a_il[:], alog[:], AF.Exp, bias=0.0, scale=1.0))

        ail_ac = s_ac.v

        @blk.vector
        def _(v):
            v.wait_ge(s_ac.h, ail_ac)
            s_dv.inc(nc.vector.tensor_scalar(a_il[:], a_il[:], -1.0, None, op0=OP.mult))

        if "A" in phases:
            import os as _os
            SERIAL = _os.environ.get("SERIAL", "0") == "1"

            def allwait(eng):
                if SERIAL:
                    eng.wait_ge(s_ld.h, s_ld.v)
                    eng.wait_ge(s_gld.h, s_gld.v)
                    eng.wait_ge(s_pe.h, s_pe.v)
                    eng.wait_ge(s_dv.h, s_dv.v)
                    eng.wait_ge(s_ac.h, s_ac.v)
                    eng.wait_ge(s_st.h, s_st.v)
            if _os.environ.get("DEDIC", "0") == "1":
                xtile = [sb("ded_xt0", (128, 1024), F32)[:], sb("ded_xt1", (128, 1024), F32)[:]]
                sqbuf = sb("ded_sq", (128, 1024), F32)[:]
                xstage = sb("ded_xs", (128, 512), BF16)[:]
                stats = [sb("ded_st0", (128, 12), F32)[:], sb("ded_st1", (128, 12), F32)[:]]
                xnbuf = sb("ded_xn", (128, 1024), BF16)[:]
            else:
                xtile = [kview(0, (128, 1024), F32), kview(4096, (128, 1024), F32)]
                sqbuf = kview(8192, (128, 1024), F32)
                xstage = kview(12288, (128, 512), BF16)
                stats = [kview(13312, (128, 12), F32), kview(13360, (128, 12), F32)]
                xnbuf = kview(13440, (128, 1024), BF16)
            _ps = contextlib.ExitStack()
            psT = _ps.enter_context(nc.psum_tensor("ps_auto1", [128, 1024], BF16))[:]
            psX = _ps.enter_context(nc.psum_tensor("ps_auto2", [128, 512], F32))[:]
            pe_done = []

            for i in range(NT32):
                par = i % 2
                st_ = stats[par]

                @blk.sync
                def _(q, i=i, par=par):
                    if i >= 2:
                        q.wait_ge(s_pe.h, pe_done[i - 2])
                    s_ld.dma(q.dma_start(out=xtile[par], in_=x_d[i * 128:(i + 1) * 128, :]))

                ld_here = s_ld.v

                @blk.vector
                def _(v, i=i, par=par, st_=st_):
                    allwait(v)
                    v.wait_ge(s_ld.h, ld_here)
                    xt = xtile[par]
                    nc.vector.tensor_reduce(st_[:, 0:1], xt, axis=AX.X, op=OP.add)
                    nc.vector.tensor_tensor(sqbuf[:], xt, xt, op=OP.mult)
                    s_dv.inc(nc.vector.tensor_reduce(st_[:, 1:2], sqbuf[:], axis=AX.X, op=OP.add))
                    v.wait_ge(s_dv.h, s_dv.v)
                    nc.vector.tensor_scalar(st_[:, 2:3], st_[:, 0:1], 1.0 / D, None, op0=OP.mult)
                    s_dv.inc(nc.vector.tensor_scalar(st_[:, 3:4], st_[:, 1:2], 1.0 / D, None, op0=OP.mult))
                    v.wait_ge(s_dv.h, s_dv.v)
                    s_dv.inc(nc.vector.scalar_tensor_tensor(
                        st_[:, 5:6], st_[:, 2:3], st_[:, 2:3], st_[:, 3:4],
                        op0=OP.mult, op1=OP.subtract))

                var_dv = s_dv.v

                @blk.scalar
                def _(a, st_=st_):
                    allwait(a)
                    a.wait_ge(s_dv.h, var_dv)
                    s_ac.inc(nc.scalar.activation(st_[:, 6:7], st_[:, 5:6], AF.Ln, bias=1e-5, scale=-1.0))
                    a.wait_ge(s_ac.h, s_ac.v)
                    s_ac.inc(nc.scalar.activation(st_[:, 7:8], st_[:, 6:7], AF.Exp, bias=0.0, scale=-0.5))

                rstd_ac = s_ac.v

                @blk.vector
                def _(v, par=par, st_=st_):
                    allwait(v)
                    v.wait_ge(s_ac.h, rstd_ac)
                    v.wait_ge(s_pe.h, s_pe.v)
                    s_dv.inc(nc.vector.tensor_tensor(
                        sqbuf[:], xtile[par], st_[:, 2:3].broadcast_to([128, 1024]),
                        op=OP.subtract))
                    v.wait_ge(s_dv.h, s_dv.v)
                    s_dv.inc(nc.vector.tensor_tensor(
                        xnbuf[:], sqbuf[:], st_[:, 7:8].broadcast_to([128, 1024]),
                        op=OP.mult))

                xn_dv = s_dv.v

                @blk.tensor
                def _(t, par=par):
                    allwait(t)
                    t.wait_ge(s_dv.h, xn_dv)
                    t.wait_ge(s_dv.h, s_dv.v)
                    for fc in range(8):
                        instr = nc.tensor.transpose(psT[:, fc * 128:(fc + 1) * 128],
                                                    xnbuf[:, fc * 128:(fc + 1) * 128], identb[:])
                    for g in range(4):
                        instr = nc.tensor.transpose(psX[:, g * 128:(g + 1) * 128],
                                                    xtile[par][:, g * 128:(g + 1) * 128], identf[:])
                    s_pe.inc(instr)

                pe_done.append(s_pe.v)
                tp_pe = s_pe.v

                if debug and i < 8:
                    dbg_xnb = nc.dram_tensor(f"dbg_xnb{i}", [128, 1024], BF16,
                                             kind="ExternalOutput").ap()
                    dbg_stats = nc.dram_tensor(f"dbg_stats{i}", [128, 12], F32,
                                               kind="ExternalOutput").ap()

                    @blk.sync
                    def _(q, i=i, st_=st_):
                        q.wait_ge(s_pe.h, tp_pe)
                        s_st.dma(q.dma_start(out=dbg_xnb, in_=xnbuf[:]))
                        s_st.dma(q.dma_start(out=dbg_stats, in_=st_))

                    st_snap = s_st.v

                @blk.vector
                def _(v, i=i):
                    allwait(v)
                    if debug and i < 8:
                        v.wait_ge(s_st.h, st_snap)
                    v.wait_ge(s_pe.h, tp_pe)
                    if i >= 1:
                        v.wait_ge(s_st.h, s_st.v)
                    nc.vector.tensor_copy(
                        xn_lo[:, :, i * 128:(i + 1) * 128],
                        psT[:, 0:512].rearrange("p (f t) -> p f t", t=128))
                    nc.vector.tensor_copy(xstage[:], psT[:, 512:1024])
                    s_dv.inc(nc.vector.tensor_copy(
                        oa[:, :, i * 128:(i + 1) * 128],
                        psX[:].rearrange("p (g t) -> p g t", t=128)))

                ev_dv = s_dv.v

                @blk.sync
                def _(q, i=i):
                    q.wait_ge(s_dv.h, ev_dv)
                    s_st.dma(q.dma_start(
                        out=xnup_dv[:, :, i * 128:(i + 1) * 128],
                        in_=xstage[:].rearrange("p (f t) -> p f t", t=128)))

    try:
        _ps.close()
    except NameError:
        pass

    with nc.Block() as blk:
        @blk.gpsimd
        def _(g):
            for dst, src in [
                (wb_bf[:], wb_d), (wc_bf[:], wc_d),
                (wg_bf[:], wg_d), (bexp_bf[:], bexp_d),
            ]:
                s_gld.dma(g.dma_start(out=dst, in_=src))

    wcast_gld = s_gld.v
    xnup_done = s_st.v

    if debug:
        with nc.Block() as blk:
            dbg_xnlow = nc.dram_tensor("dbg_xnlow", [128, 4 * S], BF16, kind="ExternalOutput").ap()
            dbg_oa1 = nc.dram_tensor("dbg_oa1", [128, 4 * S], F32, kind="ExternalOutput").ap()

            @blk.sync
            def _(q):
                q.wait_ge(s_dv.h, s_dv.v)
                s_st.dma(q.dma_start(out=dbg_xnlow, in_=xnlow[:, 0:4 * S]))
                s_st.dma(q.dma_start(out=dbg_oa1, in_=out_acc[:]))
                q.wait_ge(s_st.h, s_st.v)
    xnup_done = s_st.v

    if "B" in phases:
        with nc.Block() as blk:
            xup = [kview(0, (128, 4, 512), BF16), kview(4096, (128, 4, 512), BF16)]
            zstage = kview(8192, (8, 512), F32)
            zt_sb = kview(10240, (128, 4, 8), F32)
            combT = kview(10368, (128, 4, 8), F32)
            cstage = kview(10496, (8, 512), F32)
            bstage = kview(12544, (16, 512), F32)
            cstage2 = kview(14592, (16, 512), F32)
            gtmp = kview(16640, (128, 4, 16), F32)
            egbuf = kview(16896 + 128, (128, 4, 8), F32)
            _psB = contextlib.ExitStack()
            psZ = _psB.enter_context(nc.psum_tensor("ps_auto3", [16, 512], F32))[:]
            psZT = _psB.enter_context(nc.psum_tensor("ps_auto4", [128, 32], F32))[:]
            psBC = _psB.enter_context(nc.psum_tensor("ps_auto5", [16, 1024], F32))[:]
            psCT = _psB.enter_context(nc.psum_tensor("ps_auto6", [8, 512], F32))[:]
            pe_done = []

            for tc in range(NTC):
                par = tc % 2

                @blk.gpsimd
                def _(g, tc=tc, par=par):
                    if tc == 0:
                        g.wait_ge(s_st.h, xnup_done)
                    if tc >= 2:
                        g.wait_ge(s_pe.h, pe_done[3 * (tc - 2)])
                    s_gld.dma(g.dma_start(out=xup[par],
                                          in_=xnup_dv[:, :, tc * 512:(tc + 1) * 512]))

                ld_here = s_gld.v

                @blk.tensor
                def _(t, tc=tc, par=par):
                    t.wait_ge(s_gld.h, ld_here)
                    if tc == 0:
                        t.wait_ge(s_ld.h, s_ld.v)
                    t.wait_ge(s_dv.h, s_dv.v)
                    wgv = wg_bf[:].rearrange("p (f e) -> p f e", e=NE)
                    wbv = wb_bf[:].rearrange("p (f n) -> p f n", n=NST)
                    wcv = wc_bf[:].rearrange("p (f n) -> p f n", n=NST)
                    for fc in range(8):
                        rhs = xn_lo[:, fc, tc * 512:(tc + 1) * 512] if fc < 4 else xup[par][:, fc - 4, :]
                        nc.tensor.matmul(psZ[0:8, :], wgv[:, fc, :], rhs,
                                         start=(fc == 0), stop=(fc == 7))
                        nc.tensor.matmul(psBC[:, 0:512], wbv[:, fc, :], rhs,
                                         start=(fc == 0), stop=(fc == 7))
                        instr = nc.tensor.matmul(psBC[:, 512:1024], wcv[:, fc, :], rhs,
                                                 start=(fc == 0), stop=(fc == 7))
                    s_pe.inc(instr)

                pe_done.append(s_pe.v)
                mm_pe = s_pe.v

                @blk.vector
                def _(v, tc=tc):
                    v.wait_ge(s_pe.h, mm_pe)
                    if tc >= 1:
                        v.wait_ge(s_st.h, s_st.v)
                    nc.vector.tensor_scalar(zstage[:], psZ[0:8, :], bgv[:], None, op0=OP.add)
                    nc.vector.tensor_scalar(bstage[:], psBC[:, 0:512], bbv[:], None, op0=OP.add)
                    s_dv.inc(nc.vector.tensor_scalar(cstage2[:], psBC[:, 512:1024], bcv[:], None, op0=OP.add))

                ev_dv = s_dv.v

                @blk.sync
                def _(q, tc=tc):
                    q.wait_ge(s_dv.h, ev_dv)
                    s_st.dma(q.dma_start(out=bT_d[:, tc * 512:(tc + 1) * 512], in_=bstage[:]))
                    s_st.dma(q.dma_start(out=cT_d[:, tc * 512:(tc + 1) * 512], in_=cstage2[:]))

                if "3" in bparts:
                    @blk.vector
                    def _(v, tc=tc):
                        for g in range(4):
                            s_dv.inc(nc.vector.scalar_tensor_tensor(
                                oa[:, g, tc * 512:(tc + 1) * 512],
                                xn_lo[:, g, tc * 512:(tc + 1) * 512], dskip[:, g:g + 1],
                                oa[:, g, tc * 512:(tc + 1) * 512],
                                op0=OP.mult, op1=OP.add))

                if "2" not in bparts:
                    pe_done.append(s_pe.v)
                    pe_done.append(s_pe.v)
                    continue

                @blk.tensor
                def _(t, tc=tc):
                    t.wait_ge(s_dv.h, ev_dv)
                    for j in range(4):
                        instr = nc.tensor.transpose(psZT[:, j * 8:(j + 1) * 8],
                                                    zstage[0:8, j * 128:(j + 1) * 128],
                                                    identf[0:8, 0:8])
                    s_pe.inc(instr)

                pe_done.append(s_pe.v)
                zt_pe = s_pe.v

                @blk.vector
                def _(v, tc=tc):
                    v.wait_ge(s_pe.h, zt_pe)
                    s_dv.inc(nc.vector.tensor_copy(zt_sb[:], psZT[:].rearrange("p (j e) -> p j e", e=8)))
                    v.wait_ge(s_dv.h, s_dv.v)
                    for j in range(4):
                        s_dv.inc(nc.vector.tensor_reduce(
                            gtmp[:, j, 0:1], zt_sb[:, j, :], axis=AX.X, op=OP.max, negate=True))

                nmax_dv = s_dv.v

                @blk.vector
                def _(v, tc=tc):
                    v.wait_ge(s_dv.h, nmax_dv)
                    for j in range(4):
                        s_dv.inc(nc.vector.tensor_scalar(
                            egbuf[:, j, :], zt_sb[:, j, :], gtmp[:, j, 0:1], None,
                            op0=OP.add))

                zms_dv = s_dv.v

                @blk.scalar
                def _(a, tc=tc, zms_dv=zms_dv):
                    a.wait_ge(s_dv.h, zms_dv)
                    for j in range(4):
                        s_ac.inc(nc.scalar.activation(
                            egbuf[:, j, :], egbuf[:, j, :], AF.Exp,
                            bias=0.0, scale=1.0, accum_out=gtmp[:, j, 1:2]))

                eg_ac = s_ac.v

                @blk.vector
                def _(v, tc=tc):
                    v.wait_ge(s_ac.h, eg_ac)
                    for j in range(4):
                        nc.vector.reciprocal(gtmp[:, j, 2:3], gtmp[:, j, 1:2])
                        s_dv.inc(nc.vector.tensor_scalar(
                            gtmp[:, j, 8:16], zt_sb[:, j, :], gtmp[:, j, 0:1], 0.0,
                            op0=OP.add, op1=OP.is_ge))
                        v.wait_ge(s_dv.h, s_dv.v)
                        s_dv.inc(nc.vector.scalar_tensor_tensor(
                            gtmp[:, j, 8:16], gtmp[:, j, 8:16], -1e30, zt_sb[:, j, :],
                            op0=OP.mult, op1=OP.add))
                        v.wait_ge(s_dv.h, s_dv.v)
                        s_dv.inc(nc.vector.tensor_reduce(
                            gtmp[:, j, 3:4], gtmp[:, j, 8:16], axis=AX.X, op=OP.max, negate=True))
                        v.wait_ge(s_dv.h, s_dv.v)
                        s_dv.inc(nc.vector.tensor_scalar(
                            gtmp[:, j, 8:16], zt_sb[:, j, :], gtmp[:, j, 3:4], 0.0,
                            op0=OP.add, op1=OP.is_ge))
                        v.wait_ge(s_dv.h, s_dv.v)
                        s_dv.inc(nc.vector.scalar_tensor_tensor(
                            combT[:, j, :], egbuf[:, j, :], gtmp[:, j, 2:3], gtmp[:, j, 8:16],
                            op0=OP.mult, op1=OP.mult))

                comb_dv = s_dv.v

                @blk.tensor
                def _(t, tc=tc):
                    t.wait_ge(s_dv.h, comb_dv)
                    for j in range(4):
                        instr = nc.tensor.transpose(psCT[:, j * 128:(j + 1) * 128],
                                                    combT[:, j, :], identf[:])
                    s_pe.inc(instr)

                pe_done.append(s_pe.v)
                ct_pe = s_pe.v

                @blk.vector
                def _(v, tc=tc):
                    v.wait_ge(s_pe.h, ct_pe)
                    s_dv.inc(nc.vector.tensor_copy(cstage[:], psCT[0:8, :]))

                cs_dv = s_dv.v

                @blk.sync
                def _(q, tc=tc):
                    q.wait_ge(s_dv.h, cs_dv)
                    s_st.dma(q.dma_start(out=comb_d[:, tc * 512:(tc + 1) * 512], in_=cstage[:]))


    try:
        _psB.close()
    except NameError:
        pass
    comb_done = s_st.v

    if "C" in phases:
        with nc.Block() as blk:
            xup = [kview(0, (128, 4, 512), BF16), kview(4096, (128, 4, 512), BF16)]
            xebuf = [vview(0, (128, 8, 512), BF16), vview(8192, (128, 8, 512), BF16)]
            cbc = [vview(16384, (128, 512), BF16), vview(17408, (128, 512), BF16)]
            _psC = contextlib.ExitStack()
            psEx = _psC.enter_context(nc.psum_tensor("ps_auto7", [128, 2048], F32))[:]
            mm_done = []
            ev_done = []
            bias_done = []

            for tc in range(NTC):
                par = tc % 2

                @blk.gpsimd
                def _(g, tc=tc, par=par):
                    if tc == 0:
                        g.wait_ge(s_st.h, comb_done)
                    if tc >= 2:
                        g.wait_ge(s_pe.h, mm_done[NE * (tc - 2) + NE - 1])
                        g.wait_ge(s_pe.h, bias_done[tc - 2])
                    s_gld.dma(g.dma_start(out=xup[par], in_=xnup_dv[:, :, tc * 512:(tc + 1) * 512]))
                    s_gld.dma(g.dma_start(out=cbrow[par][:], in_=comb_d[:, tc * 512:(tc + 1) * 512]))

                for e in range(NE):
                    epar = e % 2
                    ge = tc * NE + e

                    @blk.gpsimd
                    def _(g, tc=tc, e=e, epar=epar, ge=ge):
                        if ge >= 2:
                            g.wait_ge(s_pe.h, mm_done[ge - 2])
                        s_gld.dma(g.dma_start(out=wexp_bf[epar][:], in_=wexp_d[e]))
                        src = comb_d[e:e + 1, tc * 512:(tc + 1) * 512].partition_broadcast(128)
                        s_gld.dma(g.dma_start(out=cbc[epar][:], in_=src))

                    eld = s_gld.v

                    @blk.vector
                    def _(v, tc=tc, e=e, epar=epar, par=par, ge=ge):
                        v.wait_ge(s_gld.h, eld)
                        if ge >= 2:
                            v.wait_ge(s_pe.h, mm_done[ge - 2])
                        xe = xebuf[epar]
                        nc.vector.tensor_tensor(
                            xe[:, 0:4, :],
                            xn_lo[:, :, tc * 512:(tc + 1) * 512],
                            cbc[epar][:].unsqueeze(1).broadcast_to([128, 4, 512]),
                            op=OP.mult)
                        s_dv.inc(nc.vector.tensor_tensor(
                            xe[:, 4:8, :], xup[par],
                            cbc[epar][:].unsqueeze(1).broadcast_to([128, 4, 512]),
                            op=OP.mult))

                    xe_dv = s_dv.v

                    @blk.tensor
                    def _(t, tc=tc, e=e, epar=epar):
                        t.wait_ge(s_dv.h, xe_dv)
                        if e == 0 and tc > 0:
                            t.wait_ge(s_dv.h, ev_done[tc - 1])
                        wx = wexp_bf[epar][:].rearrange("p (f g q) -> p f g q", g=4, q=128)
                        for g in range(4):
                            for fc in range(8):
                                instr = nc.tensor.matmul(
                                    psEx[:, g * 512:(g + 1) * 512],
                                    wx[:, fc, g, :], xebuf[epar][:, fc, :],
                                    start=(e == 0 and fc == 0), stop=False)
                        s_pe.inc(instr)

                    mm_done.append(s_pe.v)

                @blk.tensor
                def _(t, tc=tc, par=par):
                    for g in range(4):
                        instr = nc.tensor.matmul(
                            psEx[:, g * 512:(g + 1) * 512],
                            bexp_bf[:, g * 128:(g + 1) * 128], cbrow[par][:],
                            start=False, stop=True)
                    s_pe.inc(instr)

                bias_done.append(s_pe.v)
                ex_pe = s_pe.v

                @blk.vector
                def _(v, tc=tc, ex_pe=ex_pe):
                    v.wait_ge(s_pe.h, ex_pe)
                    for g in range(4):
                        s_dv.inc(nc.vector.scalar_tensor_tensor(
                            oa[:, g, tc * 512:(tc + 1) * 512],
                            psEx[:, g * 512:(g + 1) * 512], 1.0,
                            oa[:, g, tc * 512:(tc + 1) * 512],
                            op0=OP.mult, op1=OP.add))

                ev_done.append(s_dv.v)

    try:
        _psC.close()
    except NameError:
        pass

    if "D" in phases:
        with nc.Block() as blk:
            TB = 512
            NTB = S // TB
            dt_sb = kview(0, (128, TB), BF16)
            e1_sb = kview(2048, (128, TB), F32)
            w_sb = kview(4096, (128, TB), BF16)
            a_sb = [kview(6144, (128, TB), F32), kview(8192, (128, TB), F32)]
            u_sb = [kview(10240, (128, TB), F32), kview(12288, (128, TB), F32)]
            h_sb = [kview(14336, (128, TB), F32), kview(16384, (128, TB), F32)]
            hc_sb = [kview(18432, (128, TB), BF16), kview(19456, (128, TB), BF16)]
            brep = [kview(20480, (128, TB), BF16), kview(21504, (128, TB), BF16)]
            crep = [kview(22528, (128, TB), BF16), kview(23552, (128, TB), BF16)]
            xupd = [kview(24576, (128, 4, TB), BF16), kview(28672, (128, 4, TB), BF16)]
            r1b = kview(30720, (128, 512), BF16)

            zred = _mkview(wexp_bf[0], 0, (128, 2048), BF16)
            r1f = _mkview(wexp_bf[1], 0, (128, 2048), BF16)

            @blk.gpsimd
            def _(g):
                s_gld.dma(g.dma_start(out=wbig[:], in_=wd_d))
                s_gld.dma(g.dma_start(out=zred, in_=zred_d))
                s_gld.dma(g.dma_start(out=r1f, in_=r1f_d))

            wdld = s_gld.v
            _psD = contextlib.ExitStack()
            psDt = _psD.enter_context(nc.psum_tensor("ps_auto8", [128, 512], F32))[:]
            psR = [_psD.enter_context(nc.psum_tensor(f"ps_psR{i}", [128, 1024], F32))[:] for i in range(2)]
            psY = _psD.enter_context(nc.psum_tensor("ps_auto10", [128, 512], F32))[:]
            prev_evac_dv = None
            it_end_dv = []
            it_end_po = []

            for g4 in range(4):
                for tb in range(NTB):
                    it = g4 * NTB + tb
                    par = it % 2

                    pe0 = s_pe.v
                    ac0 = s_ac.v
                    dv0 = s_dv.v
                    po0 = s_po.v

                    pe_wd = pe0 + 1
                    pe_repl = {}
                    pe_z = {}
                    c = pe_wd
                    order = []
                    for k in range(2):
                        c += 1; pe_repl[k] = c; order.append(("r", k))
                    for k in range(16):
                        c += 1; pe_z[k] = c; order.append(("z", k))
                        if k + 2 < 16:
                            c += 1; pe_repl[k + 2] = c; order.append(("r", k + 2))
                    pe_end = c
                    ac_sp = ac0 + 2
                    ac_exp = {k: ac_sp + k + 1 for k in range(16)}
                    dv_w = dv0 + 1
                    dv_u = {k: dv_w + 3 * k + 1 for k in range(16)}
                    dv_scan = {k: dv_w + 3 * k + 2 for k in range(16)}
                    dv_state = {k: dv_w + 3 * k + 3 for k in range(16)}
                    dv_evac = dv_w + 48 + 1
                    po_hc = {k: po0 + k + 1 for k in range(16)}

                    @blk.gpsimd
                    def _(gq, g4=g4, tb=tb, par=par, it=it, dv_end=dv_evac, po_end=po_hc[15]):
                        if it == 0:
                            gq.wait_ge(s_st.h, comb_done)
                        if it >= 2:
                            gq.wait_ge(s_dv.h, it_end_dv[it - 2])
                            gq.wait_ge(s_po.h, it_end_po[it - 2])
                        s_gld.dma(gq.dma_start(out=xupd[par],
                                               in_=xnup_dv[:, :, tb * TB:(tb + 1) * TB]))
                        bsrc = bT_d[:, tb * TB:(tb + 1) * TB].unsqueeze(0).broadcast_to([8, NST, TB])
                        csrc = cT_d[:, tb * TB:(tb + 1) * TB].unsqueeze(0).broadcast_to([8, NST, TB])
                        s_gld.dma(gq.dma_start(out=brep[par][:], in_=bsrc))
                        s_gld.dma(gq.dma_start(out=crep[par][:], in_=csrc))
                        for k in range(16):
                            kp = k % 2
                            gq.wait_ge(s_dv.h, dv_scan[k])
                            if k >= 2:
                                gq.wait_ge(s_pe.h, pe_z[k - 2])
                            elif it >= 1:
                                gq.wait_ge(s_pe.h, pe0)
                            s_po.inc(nc.gpsimd.tensor_tensor(
                                hc_sb[kp][:], h_sb[kp][:], crep[par][:], op=OP.mult))

                    ld_here = s_gld.v

                    @blk.tensor
                    def _(t, g4=g4, tb=tb, par=par, it=it, ld_here=ld_here):
                        t.wait_ge(s_gld.h, ld_here)
                        t.wait_ge(s_ac.h, ac0)
                        if prev_evac_dv is not None:
                            t.wait_ge(s_dv.h, prev_evac_dv)
                        wdv = wbig[:].rearrange("p (f c) -> p f c", c=DH)
                        for fc in range(8):
                            rhs = (xn_lo[:, fc, tb * TB:(tb + 1) * TB] if fc < 4
                                   else xupd[par][:, fc - 4, :])
                            instr = nc.tensor.matmul(
                                psDt[:], wdv[:, fc, g4 * 128:(g4 + 1) * 128], rhs,
                                start=(fc == 0), stop=(fc == 7))
                        s_pe.inc(instr)
                        for kind, k in order:
                            kp = k % 2
                            r, m = k // 4, k % 4
                            if kind == "r":
                                t.wait_ge(s_dv.h, dv_w)
                                if k >= 2:
                                    t.wait_ge(s_ac.h, ac_exp[k - 2])
                                    t.wait_ge(s_dv.h, dv_u[k - 2])
                                nc.tensor.matmul(psR[kp][:, 0:512],
                                                 r1f[:, k * 128:(k + 1) * 128],
                                                 dt_sb[:], start=True, stop=True)
                                instr = nc.tensor.matmul(psR[kp][:, 512:1024],
                                                         r1f[:, k * 128:(k + 1) * 128],
                                                         w_sb[:], start=True, stop=True)
                                s_pe.inc(instr)
                            else:
                                t.wait_ge(s_po.h, po_hc[k])
                                instr = nc.tensor.matmul(
                                    psY[:], zred[:, k * 128:(k + 1) * 128], hc_sb[kp][:],
                                    start=(k == 0), stop=(k == 15))
                                s_pe.inc(instr)

                    assert s_pe.v == pe_end, (s_pe.v, pe_end)

                    @blk.scalar
                    def _(a, g4=g4, it=it):
                        a.wait_ge(s_pe.h, pe_wd)
                        if prev_evac_dv is not None:
                            a.wait_ge(s_dv.h, dv0)
                        s_ac.inc(nc.scalar.activation(e1_sb[:], psDt[:], AF.Exp,
                                                      bias=bd[:, g4:g4 + 1], scale=1.0))
                        s_ac.inc(nc.scalar.activation(dt_sb[:], e1_sb[:], AF.Ln, bias=1.0, scale=1.0))
                        for k in range(16):
                            kglob = g4 * 16 + k
                            a.wait_ge(s_pe.h, pe_repl[k])
                            if k >= 2:
                                a.wait_ge(s_dv.h, dv_scan[k - 2])
                            elif it >= 1:
                                a.wait_ge(s_dv.h, dv0)
                            s_ac.inc(nc.scalar.activation(
                                a_sb[k % 2][:], psR[k % 2][:, 0:512], AF.Exp,
                                bias=0.0, scale=a_il[:, kglob:kglob + 1]))

                    assert s_ac.v == ac_exp[15], (s_ac.v, ac_exp[15])

                    @blk.vector
                    def _(v, g4=g4, tb=tb, par=par, it=it):
                        v.wait_ge(s_ac.h, ac_sp)
                        s_dv.inc(nc.vector.tensor_tensor(
                            w_sb[:], dt_sb[:], xn_lo[:, g4, tb * TB:(tb + 1) * TB], op=OP.mult))
                        for k in range(16):
                            kp = k % 2
                            kglob = g4 * 16 + k
                            v.wait_ge(s_pe.h, pe_repl[k])
                            if k >= 2:
                                v.wait_ge(s_po.h, po_hc[k - 2])
                            elif it >= 1:
                                v.wait_ge(s_po.h, po0)
                            s_dv.inc(nc.vector.tensor_tensor(
                                u_sb[kp][:], psR[kp][:, 512:1024], brep[par][:], op=OP.mult))
                            v.wait_ge(s_ac.h, ac_exp[k])
                            v.wait_ge(s_dv.h, dv_u[k])
                            init = 0.0 if tb == 0 else state[:, kglob:kglob + 1]
                            s_dv.inc(nc.vector.tensor_tensor_scan(
                                h_sb[kp][:], a_sb[kp][:], u_sb[kp][:], init,
                                op0=OP.mult, op1=OP.add))
                            v.wait_ge(s_dv.h, dv_scan[k])
                            s_dv.inc(nc.vector.tensor_copy(state[:, kglob:kglob + 1],
                                                           h_sb[kp][:, TB - 1:TB]))
                        v.wait_ge(s_pe.h, pe_end)
                        s_dv.inc(nc.vector.scalar_tensor_tensor(
                            oa[:, g4, tb * TB:(tb + 1) * TB], psY[:], 1.0,
                            oa[:, g4, tb * TB:(tb + 1) * TB], op0=OP.mult, op1=OP.add))

                    assert s_dv.v == dv_evac, (s_dv.v, dv_evac)
                    prev_evac_dv = dv_evac
                    it_end_dv.append(s_dv.v)
                    it_end_po.append(s_po.v)

    try:
        _psD.close()
    except NameError:
        pass

    if "E" in phases:
        with nc.Block() as blk:
            xup = [_mkview(wexp_bf[0], 0, (128, 4, 512), BF16),
                   _mkview(wexp_bf[0], 4096, (128, 4, 512), BF16)]
            vstage = _mkview(wexp_bf[1], 0, (128, 4, 512), BF16)
            qstage = _mkview(wexp_bf[1], 4096, (128, 4, 512), BF16)
            _psE = contextlib.ExitStack()
            psP = _psE.enter_context(nc.psum_tensor("ps_autoP", [128, 2048], F32))[:]
            psVT = [_psE.enter_context(nc.psum_tensor(f"ps_psVT{i}", [128, 512], BF16))[:] for i in range(2)]

            @blk.gpsimd
            def _(g):
                s_po.inc(nc.gpsimd.memset(vt_h[:, :, :, 64:65], 1.0))

            ones_po = s_po.v
            proj_pe = []

            for di, dst in enumerate(("k", "v", "q")):
                w_src = {"k": wk_d, "v": wv_d, "q": wq_d}[dst]

                @blk.gpsimd
                def _(g, di=di, dst=dst, w_src=w_src):
                    if di > 0:
                        g.wait_ge(s_pe.h, proj_pe[-1])
                    s_gld.dma(g.dma_start(out=wbig[:], in_=w_src))

                wld = s_gld.v

                for tc in range(NTC):
                    par = tc % 2

                    @blk.gpsimd
                    def _(g, tc=tc, par=par):
                        if len(proj_pe) >= 2:
                            g.wait_ge(s_pe.h, proj_pe[-2])
                        s_gld.dma(g.dma_start(out=xup[par],
                                              in_=xnup_dv[:, :, tc * 512:(tc + 1) * 512]))

                    ld_here = s_gld.v

                    @blk.tensor
                    def _(t, tc=tc, par=par, wld=wld, ld_here=ld_here):
                        t.wait_ge(s_gld.h, max(wld, ld_here))
                        t.wait_ge(s_dv.h, s_dv.v)
                        w = wbig[:].rearrange("p (f c) -> p f c", c=DH)
                        for pg in range(4):
                            for fc in range(8):
                                rhs = (xn_lo[:, fc, tc * 512:(tc + 1) * 512] if fc < 4
                                       else xup[par][:, fc - 4, :])
                                instr = nc.tensor.matmul(
                                    psP[:, pg * 512:(pg + 1) * 512],
                                    w[:, fc, pg * 128:(pg + 1) * 128], rhs,
                                    start=(fc == 0), stop=(fc == 7))
                        s_pe.inc(instr)

                    proj_pe.append(s_pe.v)
                    mm_pe = s_pe.v

                    if dst == "k":
                        @blk.vector
                        def _(v, tc=tc, mm_pe=mm_pe):
                            v.wait_ge(s_pe.h, mm_pe)
                            for pg in range(4):
                                s_dv.inc(nc.vector.tensor_scalar(
                                    KT_v[:, pg, tc * 512:(tc + 1) * 512],
                                    psP[:, pg * 512:(pg + 1) * 512],
                                    bk[:, pg:pg + 1], None, op0=OP.add))
                    elif dst == "v":
                        @blk.vector
                        def _(v, tc=tc, mm_pe=mm_pe):
                            v.wait_ge(s_pe.h, mm_pe)
                            for pg in range(4):
                                s_dv.inc(nc.vector.tensor_scalar(
                                    vstage[:, pg, :], psP[:, pg * 512:(pg + 1) * 512],
                                    bv[:, pg:pg + 1], None, op0=OP.add))

                        vs_dv = s_dv.v

                        for pg in range(4):
                            @blk.tensor
                            def _(t, tc=tc, pg=pg, vs_dv=vs_dv):
                                t.wait_ge(s_dv.h, s_dv.v)
                                if tc == 0 and pg == 0:
                                    t.wait_ge(s_po.h, ones_po)
                                for j in range(4):
                                    instr = nc.tensor.transpose(
                                        psVT[pg % 2][:, j * 128:(j + 1) * 128],
                                        vstage[:, pg, :][:, j * 128:(j + 1) * 128], identb[:])
                                s_pe.inc(instr)

                            vt_pe = s_pe.v

                            @blk.vector
                            def _(v, tc=tc, pg=pg, vt_pe=vt_pe):
                                v.wait_ge(s_pe.h, vt_pe)
                                s_dv.inc(nc.vector.tensor_copy(
                                    vt_h[:, tc * 4:(tc + 1) * 4, 2 * pg:2 * pg + 2, 0:64],
                                    psVT[pg % 2][:].rearrange("p (a h j) -> p a h j", h=2, j=64)))
                    else:
                        @blk.vector
                        def _(v, tc=tc, mm_pe=mm_pe):
                            v.wait_ge(s_pe.h, mm_pe)
                            if tc >= 1:
                                v.wait_ge(s_st.h, s_st.v)
                            for pg in range(4):
                                s_dv.inc(nc.vector.tensor_scalar(
                                    qstage[:, pg, :], psP[:, pg * 512:(pg + 1) * 512],
                                    bq[:, pg:pg + 1], None, op0=OP.add))

                        q_dv = s_dv.v

                        @blk.sync
                        def _(q, tc=tc, q_dv=q_dv):
                            q.wait_ge(s_dv.h, q_dv)
                            s_st.dma(q.dma_start(
                                out=qt_dv[:, :, tc * 512:(tc + 1) * 512], in_=qstage[:]))

    try:
        _psE.close()
    except NameError:
        pass
    qt_done = s_st.v

    if "F" in phases:
        with nc.Block() as blk:
            NKT = S // 128
            NQC = S // 512
            pT = [[xview(0, (128, 1024), BF16), xview(2048, (128, 1024), BF16)],
                  [xview(4096, (128, 1024), BF16), xview(6144, (128, 1024), BF16)]]
            attnU = xview(8192, (128, NQC * 512), BF16)
            tmpn = xview(8192 + NQC * 1024, (128, 512), F32)
            stage64 = xview(8192 + NQC * 1024 + 2048, (128, 1024), F32)
            dtile = xview(8192 + NQC * 1024 + 6144, (16, 512), F32)
            rdtile = xview(8192 + NQC * 1024 + 8192, (16, 512), F32)
            tmpb = xview(8192 + NQC * 1024 + 10240, (128, 512), BF16)
            _psF = contextlib.ExitStack()
            psS = [_psF.enter_context(nc.psum_tensor(f"ps_psS{i}", [128, 512], F32))[:] for i in range(4)]
            psO = [_psF.enter_context(nc.psum_tensor(f"ps_psO{i}", [128, 512], F32))[:] for i in range(2)]
            psD = _psF.enter_context(nc.psum_tensor("ps_auto15", [128, 512], F32))[:]
            exp_ac = {}
            pv_pe = {}
            nrm_dv = [0]
            nrm_pe = {}

            for pr in range(4):
                @blk.sync
                def _(q, pr=pr):
                    if pr == 0:
                        q.wait_ge(s_st.h, qt_done)
                    if pr >= 1:
                        q.wait_ge(s_dv.h, nrm_dv[-1])
                    s_st.dma(q.dma_start(out=qt_pair[:], in_=qt_dv[:, pr, :]))

                qld = s_st.v

                den_dma = []
                evac_dv = []
                for qc in range(NQC):
                    pe0 = s_pe.v
                    ac0 = s_ac.v
                    pe_sc = {}
                    pe_pv = {}
                    c = pe0
                    order = []
                    for kt in range(2):
                        for hh in range(2):
                            c += 1; pe_sc[(kt, hh)] = c; order.append(("s", kt, hh))
                    for kt in range(NKT):
                        for hh in range(2):
                            c += 1; pe_pv[(kt, hh)] = c; order.append(("p", kt, hh))
                        if kt + 2 < NKT:
                            for hh in range(2):
                                c += 1; pe_sc[(kt + 2, hh)] = c; order.append(("s", kt + 2, hh))
                    pe_end = c
                    ac_exp = {}
                    c = ac0
                    for kt in range(NKT):
                        for hh in range(2):
                            c += 1; ac_exp[(kt, hh)] = c

                    @blk.tensor
                    def _(t, pr=pr, qc=qc, order=order, pe_sc=pe_sc, ac_exp=ac_exp,
                          qld=qld, evac_dv=tuple(evac_dv)):
                        if qc == 0:
                            t.wait_ge(s_st.h, qld)
                        for kind, kt, hh in order:
                            kp = kt % 2
                            if kind == "s":
                                if kt >= 2:
                                    t.wait_ge(s_ac.h, ac_exp[(kt - 2, hh)])
                                elif qc >= 1:
                                    t.wait_ge(s_ac.h, ac0)
                                instr = nc.tensor.matmul(
                                    psS[2 * (kt % 2) + hh][:, 0:512],
                                    KT_v[64 * hh:64 * (hh + 1), pr, kt * 128:(kt + 1) * 128],
                                    qt_pair[64 * hh:64 * (hh + 1), qc * 512:(qc + 1) * 512],
                                    start=True, stop=True, tile_position=(64 * hh, 0))
                                s_pe.inc(instr)
                            else:
                                t.wait_ge(s_ac.h, ac_exp[(kt, hh)])
                                if kt == 0 and qc >= 1:
                                    t.wait_ge(s_dv.h, evac_dv[-1])
                                h_core = 2 * pr + hh
                                instr = nc.tensor.matmul(
                                    psO[hh][0:65, :],
                                    vt_v[:, kt, 65 * h_core:65 * h_core + 65],
                                    pT[hh][kt % 2][:, 0:512],
                                    start=(kt == 0), stop=(kt == NKT - 1))
                                s_pe.inc(instr)

                    assert s_pe.v == pe_end

                    @blk.scalar
                    def _(a, pr=pr, qc=qc, pe_sc=pe_sc, pe_pv=pe_pv):
                        for kt in range(NKT):
                            for hh in range(2):
                                a.wait_ge(s_pe.h, pe_sc[(kt, hh)])
                                if kt >= 2:
                                    a.wait_ge(s_pe.h, pe_pv[(kt - 2, hh)])
                                s_ac.inc(nc.scalar.activation(
                                    pT[hh][kt % 2][:, 0:512], psS[2 * (kt % 2) + hh][:, 0:512],
                                    AF.Exp, bias=-4.0, scale=0.125))

                    assert s_ac.v == ac_exp[(NKT - 1, 1)]

                    o_pe = s_pe.v

                    @blk.vector
                    def _(v, pr=pr, qc=qc, o_pe=o_pe):
                        v.wait_ge(s_pe.h, o_pe)
                        if qc >= 1:
                            v.wait_ge(s_st.h, den_dma[-1])
                        nc.vector.tensor_copy(attnU[0:64, qc * 512:(qc + 1) * 512], psO[0][0:64, :])
                        nc.vector.tensor_copy(tmpb[0:64, :], psO[1][0:64, :])
                        nc.vector.tensor_copy(stage64[64:65, 0:512], psO[0][64:65, :])
                        s_dv.inc(nc.vector.tensor_copy(stage64[64:65, 512:1024], psO[1][64:65, :]))

                    st_dv = s_dv.v

                    @blk.tensor
                    def _(t, qc=qc, st_dv=st_dv):
                        t.wait_ge(s_dv.h, st_dv)
                        s_pe.inc(nc.tensor.matmul(
                            psD[64:128, :], identb[0:64, 0:64], tmpb[0:64, :],
                            start=True, stop=True, tile_position=(0, 64)))

                    sh_pe = s_pe.v

                    @blk.vector
                    def _(v, qc=qc, sh_pe=sh_pe):
                        v.wait_ge(s_pe.h, sh_pe)
                        s_dv.inc(nc.vector.tensor_copy(
                            attnU[64:128, qc * 512:(qc + 1) * 512], psD[64:128, :]))

                    evac_dv.append(s_dv.v)

                    @blk.sync
                    def _(q, pr=pr, qc=qc, st_dv=st_dv):
                        q.wait_ge(s_dv.h, st_dv)
                        s_st.dma(q.dma_start(
                            out=den_d[2 * pr * 8 + qc:2 * pr * 8 + qc + 1, :],
                            in_=stage64[64:65, 0:512]))
                        s_st.dma(q.dma_start(
                            out=den_d[(2 * pr + 1) * 8 + qc:(2 * pr + 1) * 8 + qc + 1, :],
                            in_=stage64[64:65, 512:1024]))

                    den_dma.append(s_st.v)

                @blk.sync
                def _(q, pr=pr):
                    q.wait_ge(s_st.h, den_dma[-1])
                    s_ld.dma(q.dma_start(out=dtile[:], in_=den_d[pr * 16:(pr + 1) * 16, :]))

                dt_ld = s_ld.v

                @blk.vector
                def _(v, pr=pr, dt_ld=dt_ld):
                    v.wait_ge(s_ld.h, dt_ld)
                    s_dv.inc(nc.vector.reciprocal(rdtile[:], dtile[:]))

                rd_dv = s_dv.v

                @blk.sync
                def _(q, pr=pr, rd_dv=rd_dv):
                    q.wait_ge(s_dv.h, rd_dv)
                    s_st.dma(q.dma_start(out=rden_d[pr * 16:(pr + 1) * 16, :], in_=rdtile[:]))

                rd_st = s_st.v

                for qc in range(NQC):
                    sl0 = 0
                    sl1 = 1

                    @blk.gpsimd
                    def _(g, pr=pr, qc=qc, sl0=sl0, sl1=sl1, rd_st=rd_st):
                        g.wait_ge(s_st.h, rd_st)
                        if qc >= 1 and (pr, qc - 1) in nrm_pe:
                            g.wait_ge(s_pe.h, nrm_pe[(pr, qc - 1)])
                        s_gld.dma(g.dma_start(
                            out=srow[0:1, sl0 * 512:(sl0 + 1) * 512],
                            in_=rden_d[2 * pr * 8 + qc:2 * pr * 8 + qc + 1, :]))
                        s_gld.dma(g.dma_start(
                            out=srow[0:1, sl1 * 512:(sl1 + 1) * 512],
                            in_=rden_d[(2 * pr + 1) * 8 + qc:(2 * pr + 1) * 8 + qc + 1, :]))

                    sr_gld = s_gld.v

                    @blk.tensor
                    def _(t, qc=qc, sl0=sl0, sl1=sl1, sr_gld=sr_gld):
                        t.wait_ge(s_gld.h, sr_gld)
                        t.wait_ge(s_dv.h, s_dv.v)
                        nc.tensor.matmul(
                            psD[0:64, :], onesb[0:1, 0:64],
                            srow[0:1, sl0 * 512:(sl0 + 1) * 512],
                            start=True, stop=True, tile_position=(0, 0))
                        instr = nc.tensor.matmul(
                            psD[64:128, :], onesb[0:1, 0:64],
                            srow[0:1, sl1 * 512:(sl1 + 1) * 512],
                            start=True, stop=True, tile_position=(0, 64))
                        nrm_pe[(pr, qc)] = s_pe.inc(instr)

                    nr_pe = s_pe.v

                    @blk.vector
                    def _(v, pr=pr, qc=qc, nr_pe=nr_pe):
                        v.wait_ge(s_pe.h, nr_pe)
                        s_dv.inc(nc.vector.tensor_tensor(
                            tmpn[:], attnU[:, qc * 512:(qc + 1) * 512], psD[:], op=OP.mult))
                        v.wait_ge(s_dv.h, s_dv.v)
                        s_dv.inc(nc.vector.tensor_tensor(
                            oa[:, pr, qc * 512:(qc + 1) * 512],
                            oa[:, pr, qc * 512:(qc + 1) * 512], tmpn[:], op=OP.add))

                nrm_dv.append(s_dv.v)

    try:
        _psF.close()
    except NameError:
        pass

    with nc.Block() as blk:
        @blk.sync
        def _(q):
            q.wait_ge(s_dv.h, s_dv.v)
            q.wait_ge(s_pe.h, s_pe.v)
            q.wait_ge(s_ac.h, s_ac.v)
            q.wait_ge(s_po.h, s_po.v)
            for g in range(4):
                s_st.dma(q.dma_start(out=out_d[:, g * S:(g + 1) * S],
                                     in_=out_acc[:, g * S:(g + 1) * S]))
            if debug:
                dap_st = nc.dram_tensor("dbg_state", [128, 64], F32, kind="ExternalOutput").ap()
                s_st.dma(q.dma_start(out=dap_st, in_=state[:]))
                for nm, src, shape, dt in [
                    ("xnup", xnup_d, (128, 4 * S), BF16),
                    ("comb", comb_d, (NE, S), F32),
                    ("bT", bT_d, (NST, S), F32),
                    ("cT", cT_d, (NST, S), F32),
                    ("qt", qt_d, (128, 4 * S), BF16),
                    ("den", den_d, (64, 512), F32),
                ]:
                    dap = nc.dram_tensor("dbg_" + nm, list(shape), dt,
                                         kind="ExternalOutput").ap()
                    s_st.dma(q.dma_start(out=dap, in_=src))
            q.wait_ge(s_st.h, s_st.v)

    return nc


def _prep_core_inputs(inputs, c, S):
    b, h = c // 2, c % 2
    own = slice(h * DH, (h + 1) * DH)
    perm = np.r_[h * DH:(h + 1) * DH, (1 - h) * DH:(2 - h) * DH]

    x = np.ascontiguousarray(inputs["L0_attn_x"][b, :S, :][:, perm], np.float32)

    Wd = inputs["L0_attn_Wd"][perm][:, own]
    Wb = inputs["L0_attn_Wb"][perm]
    Wc = inputs["L0_attn_Wc"][perm]
    Wg = inputs["L0_ffn_W_gate"][perm]
    Wqkv = inputs["L0_ffn_W_qkv"]
    Wq = Wqkv[perm][:, 0 * D:1 * D][:, own]
    Wk = Wqkv[perm][:, 1 * D:2 * D][:, own]
    Wv = Wqkv[perm][:, 2 * D:3 * D][:, own]
    bqkv = inputs["L0_ffn_b_qkv"]
    Wexp = inputs["L0_ffn_W_experts"][:, perm][:, :, own]
    bexp = inputs["L0_ffn_b_experts"][:, own]

    col4 = lambda v: np.ascontiguousarray(np.asarray(v, np.float32).reshape(4, 128).T)
    f8 = lambda w, n: np.ascontiguousarray(
        np.asarray(w, np.float32).reshape(8, 128, n).transpose(1, 0, 2))

    A_own = np.asarray(inputs["L0_attn_A_log"][own], np.float32)
    alog = np.ascontiguousarray(
        A_own.reshape(64, 8, NST).transpose(1, 2, 0).reshape(128, 64))

    zr = np.zeros((128, 2048), np.float32)
    for k16 in range(16):
        for p in range(128):
            zr[p, k16 * 128 + 8 * k16 + p // NST] = 1.0
    r1f = np.zeros((128, 2048), np.float32)
    for k16 in range(16):
        for p in range(128):
            r1f[8 * k16 + p // NST, k16 * 128 + p] = 1.0

    return dict(
        x=x,
        identf=np.eye(128, dtype=np.float32),
        identb=np.eye(128, dtype=ml_dtypes.bfloat16),
        onesb=np.ones((1, 128), ml_dtypes.bfloat16),
        zred=zr.astype(ml_dtypes.bfloat16),
        r1f=r1f.astype(ml_dtypes.bfloat16),
        alog=alog,
        dskip=col4(inputs["L0_attn_D_skip"][own]),
        wd=f8(Wd, DH), bd=col4(inputs["L0_attn_bd"][own]),
        wb=f8(Wb, NST),
        bb=np.ascontiguousarray(np.asarray(inputs["L0_attn_bb"], np.float32).reshape(NST, 1)),
        wc=f8(Wc, NST),
        bc=np.ascontiguousarray(np.asarray(inputs["L0_attn_bc"], np.float32).reshape(NST, 1)),
        wg=f8(Wg, NE),
        bg=np.ascontiguousarray(np.asarray(inputs["L0_ffn_b_gate"], np.float32).reshape(NE, 1)),
        wq=f8(Wq, DH), wk=f8(Wk, DH), wv=f8(Wv, DH),
        bq=col4(bqkv[0 * D:1 * D][own]),
        bk=col4(bqkv[1 * D:2 * D][own]),
        bv=col4(bqkv[2 * D:3 * D][own]),
        wexp=np.ascontiguousarray(
            np.asarray(Wexp, np.float32).reshape(NE, 8, 128, 4, 128).transpose(0, 2, 1, 3, 4)),
        bexp=np.ascontiguousarray(bexp, np.float32),
        cvals=np.ascontiguousarray(np.broadcast_to(
            np.asarray(CONST_VALS, np.float32), (128, 8))),
    )


def _assemble(results, S):
    out = np.empty((4, S, D), np.float32)
    for c in range(8):
        b, h = c // 2, c % 2
        shard = np.asarray(results[c]["out"])
        shard = shard.reshape(128, 4, S).transpose(1, 0, 2).reshape(DH, S)
        out[b, :, h * DH:(h + 1) * DH] = shard.T
    return out


def kernel(**inputs):
    S = inputs["L0_attn_x"].shape[1]
    nc = build_nc(S)
    in_maps = [_prep_core_inputs(inputs, c, S) for c in range(8)]
    res = run_bass_kernel_spmd(nc, in_maps, core_ids=list(range(8)))
    return _assemble(res.results, S)


# version 04
# speedup vs baseline: 2.1475x; 1.0324x over previous
import contextlib
import numpy as np
import ml_dtypes
import concourse.bass as bass
import concourse.mybir as mybir
from concourse.bass_utils import run_bass_kernel_spmd
from concourse.alu_op_type import AluOpType as OP

F32 = mybir.dt.float32
F32R = mybir.dt.float32r
BF16 = mybir.dt.bfloat16
AF = mybir.ActivationFunctionType
AX = mybir.AxisListType

D = 1024
DH = 512
NHEAD = 8
HD = 64
NE = 8
NST = 16

PHASES = "ABCDEFG"
CONST_VALS = [1e-5, -0.5, -4.0, 0.125, 1.0, 0.0, -1.0, 2.0]


class Sem:
    def __init__(self, nc, name):
        self.h = nc.alloc_semaphore(name)
        self.v = 0

    def inc(self, instr, n=1):
        instr.then_inc(self.h, n)
        self.v += n
        return self.v

    def dma(self, instr):
        return self.inc(instr, 16)


def _mkview(base, byte_off, shape, dt):
    ap = base[:, byte_off // 2:]
    ap = ap.bitcast(dt)
    n = int(np.prod(shape[1:]))
    ap = ap[0:shape[0], 0:n]
    if len(shape) == 3:
        ap = ap.rearrange("p (a b) -> p a b", b=shape[2])
    return ap


def build_nc(S=4096, phases=PHASES, debug=False, bparts="123"):
    NTC = S // 512
    NT32 = S // 128

    nc = bass.Bass("TRN2", target_bir_lowering=False)

    def din(name, shape, dt=F32):
        return nc.dram_tensor(name, list(shape), dt, kind="ExternalInput").ap()

    x_d = din("x", (S, D))
    identf_d = din("identf", (128, 128))
    identb_d = din("identb", (128, 128), BF16)
    onesb_d = din("onesb", (1, 128), BF16)
    r1f_d = din("r1f", (128, 2048), BF16)
    zred_d = din("zred", (128, 2048), BF16)
    alog_d = din("alog", (128, 64))
    dskip_d = din("dskip", (128, 4))
    wd_d = din("wd", (128, 8, DH))
    bd_d = din("bd", (128, 4))
    wb_d = din("wb", (128, 8, NST))
    bb_d = din("bb", (NST, 1))
    wc_d = din("wc", (128, 8, NST))
    bc_d = din("bc", (NST, 1))
    wg_d = din("wg", (128, 8, NE))
    bg_d = din("bg", (NE, 1))
    wq_d = din("wq", (128, 8, DH))
    wk_d = din("wk", (128, 8, DH))
    wv_d = din("wv", (128, 8, DH))
    bq_d = din("bq", (128, 4))
    bk_d = din("bk", (128, 4))
    bv_d = din("bv", (128, 4))
    wexp_d = din("wexp", (NE, 128, 8, 4, 128))
    bexp_d = din("bexp", (NE, DH))
    cvals_d = din("cvals", (128, 8))

    out_d = nc.dram_tensor("out", [128, 4 * S], F32, kind="ExternalOutput").ap()

    def dscr(name, shape, dt=F32):
        return nc.dram_tensor(name, list(shape), dt).ap()

    xnup_d = dscr("xnup_s", (128, 4 * S), BF16)
    comb_d = dscr("comb_s", (NE, S))
    bT_d = dscr("bT_s", (NST, S), BF16)
    cT_d = dscr("cT_s", (NST, S), BF16)
    qt_d = dscr("qt_s", (128, 4 * S), BF16)
    den_d = dscr("den_s", (64, 512))
    rden_d = dscr("rden_s", (64, 512))

    sb = lambda n, s, d=F32: nc.alloc_sbuf_tensor(n, list(s), d)
    out_acc = sb("out_acc", (128, 4 * S))
    xnlow = sb("xnlow", (128, max(4 * S, 16384)), BF16)
    KT = sb("KT", (128, max(4 * S, 16384)), BF16)
    vtok = sb("vtok", (128, max(NT32 * 520, 16384)), BF16)
    qt_pair = sb("qt_pair", (128, S), BF16)
    identf = sb("sb_identf", (128, 128))
    identb = sb("sb_identb", (128, 128), BF16)
    onesb = sb("sb_onesb", (1, 128), BF16)

    alog = sb("sb_alog", (128, 64))
    a_il = sb("a_il", (128, 64))
    dskip = sb("sb_dskip", (128, 4))
    bd = sb("sb_bd", (128, 4))
    bbv = sb("bbv", (NST, 1))
    bcv = sb("bcv", (NST, 1))
    bgv = sb("bgv", (NE, 1))
    bq = sb("sb_bq", (128, 4))
    bk = sb("sb_bk", (128, 4))
    bv = sb("sb_bv", (128, 4))
    state = sb("state", (128, 64))
    wbig = sb("wbig", (128, 8 * DH), BF16)
    wb_bf = sb("wb_bf", (128, 8 * NST), BF16)
    wc_bf = sb("wc_bf", (128, 8 * NST), BF16)
    wg_bf = sb("wg_bf", (128, 8 * NE), BF16)
    bexp_bf = sb("bexp_bf", (NE, DH), BF16)
    wexp_bf = [sb(f"wexp_bf{i}", (128, 4096), BF16) for i in range(2)]
    cbrow = [sb(f"cbrow{i}", (NE, 512), BF16) for i in range(2)]
    srow = sb("srow", (1, 1024), BF16)
    cvals = sb("cvals_sb", (128, 8))
    for ci, cv in enumerate(CONST_VALS):
        nc.const_aps.register(F32, cv, cvals[:, ci:ci + 1])

    kview = lambda o, s, d: _mkview(KT, o, s, d)
    xview = lambda o, s, d: _mkview(xnlow, o, s, d)
    vview = lambda o, s, d: _mkview(vtok, o, s, d)

    xn_lo = xnlow[:, 0:4 * S].rearrange("p (f t) -> p f t", t=S)
    oa = out_acc[:].rearrange("p (g t) -> p g t", t=S)
    xnup_dv = xnup_d.rearrange("p (f t) -> p f t", t=S)
    KT_v = KT[:, 0:4 * S].rearrange("p (pg t) -> p pg t", t=S)
    vt_v = vtok[:, 0:NT32 * 520].rearrange("p (tt c) -> p tt c", c=520)
    vt_h = vtok[:, 0:NT32 * 520].rearrange("p (tt h j) -> p tt h j", h=8, j=65)
    qt_dv = qt_d.rearrange("p (pg t) -> p pg t", t=S)

    s_ld = Sem(nc, "s_ld")
    s_gld = Sem(nc, "s_gld")
    s_pe = Sem(nc, "s_pe")
    s_dv = Sem(nc, "s_dv")
    s_ac = Sem(nc, "s_ac")
    s_po = Sem(nc, "s_po")
    s_st = Sem(nc, "s_st")

    with nc.Block() as blk:
        @blk.sync
        def _(q):
            for dst, src in [
                (identf[:], identf_d), (identb[:], identb_d), (onesb[:], onesb_d),
                (alog[:], alog_d),
                (dskip[:], dskip_d), (bd[:], bd_d), (bbv[:], bb_d), (bcv[:], bc_d),
                (bgv[:], bg_d), (bq[:], bq_d), (bk[:], bk_d), (bv[:], bv_d),
                (cvals[:], cvals_d),
            ]:
                s_ld.dma(q.dma_start(out=dst, in_=src))

        consts_ld = s_ld.v

        @blk.scalar
        def _(a):
            a.wait_ge(s_ld.h, consts_ld)
            s_ac.inc(nc.scalar.activation(a_il[:], alog[:], AF.Exp, bias=0.0, scale=1.0))

        ail_ac = s_ac.v

        @blk.vector
        def _(v):
            v.wait_ge(s_ac.h, ail_ac)
            s_dv.inc(nc.vector.tensor_scalar(a_il[:], a_il[:], -1.0, None, op0=OP.mult))

        if "A" in phases:
            import os as _os
            SERIAL = _os.environ.get("SERIAL", "0") == "1"

            def allwait(eng):
                if SERIAL:
                    eng.wait_ge(s_ld.h, s_ld.v)
                    eng.wait_ge(s_gld.h, s_gld.v)
                    eng.wait_ge(s_pe.h, s_pe.v)
                    eng.wait_ge(s_dv.h, s_dv.v)
                    eng.wait_ge(s_ac.h, s_ac.v)
                    eng.wait_ge(s_st.h, s_st.v)
            if _os.environ.get("DEDIC", "0") == "1":
                xtile = [sb("ded_xt0", (128, 1024), F32)[:], sb("ded_xt1", (128, 1024), F32)[:]]
                sqbuf = sb("ded_sq", (128, 1024), F32)[:]
                xstage = sb("ded_xs", (128, 512), BF16)[:]
                stats = [sb("ded_st0", (128, 12), F32)[:], sb("ded_st1", (128, 12), F32)[:]]
                xnbuf = sb("ded_xn", (128, 1024), BF16)[:]
            else:
                xtile = [kview(0, (128, 1024), F32), kview(4096, (128, 1024), F32)]
                sqbuf = kview(8192, (128, 1024), F32)
                xstage = kview(12288, (128, 512), BF16)
                stats = [kview(13312, (128, 12), F32), kview(13360, (128, 12), F32)]
                xnbuf = kview(13440, (128, 1024), BF16)
            _ps = contextlib.ExitStack()
            psT = _ps.enter_context(nc.psum_tensor("ps_auto1", [128, 1024], BF16))[:]
            psX = _ps.enter_context(nc.psum_tensor("ps_auto2", [128, 512], F32))[:]
            pe_done = []

            for i in range(NT32):
                par = i % 2
                st_ = stats[par]

                @blk.sync
                def _(q, i=i, par=par):
                    if i >= 2:
                        q.wait_ge(s_pe.h, pe_done[i - 2])
                    s_ld.dma(q.dma_start(out=xtile[par], in_=x_d[i * 128:(i + 1) * 128, :]))

                ld_here = s_ld.v

                @blk.vector
                def _(v, i=i, par=par, st_=st_):
                    allwait(v)
                    v.wait_ge(s_ld.h, ld_here)
                    xt = xtile[par]
                    nc.vector.tensor_reduce(st_[:, 0:1], xt, axis=AX.X, op=OP.add)
                    nc.vector.tensor_tensor(sqbuf[:], xt, xt, op=OP.mult)
                    s_dv.inc(nc.vector.tensor_reduce(st_[:, 1:2], sqbuf[:], axis=AX.X, op=OP.add))
                    v.wait_ge(s_dv.h, s_dv.v)
                    nc.vector.tensor_scalar(st_[:, 2:3], st_[:, 0:1], 1.0 / D, None, op0=OP.mult)
                    s_dv.inc(nc.vector.tensor_scalar(st_[:, 3:4], st_[:, 1:2], 1.0 / D, None, op0=OP.mult))
                    v.wait_ge(s_dv.h, s_dv.v)
                    s_dv.inc(nc.vector.scalar_tensor_tensor(
                        st_[:, 5:6], st_[:, 2:3], st_[:, 2:3], st_[:, 3:4],
                        op0=OP.mult, op1=OP.subtract))

                var_dv = s_dv.v

                @blk.scalar
                def _(a, st_=st_):
                    allwait(a)
                    a.wait_ge(s_dv.h, var_dv)
                    s_ac.inc(nc.scalar.activation(st_[:, 6:7], st_[:, 5:6], AF.Ln, bias=1e-5, scale=-1.0))
                    a.wait_ge(s_ac.h, s_ac.v)
                    s_ac.inc(nc.scalar.activation(st_[:, 7:8], st_[:, 6:7], AF.Exp, bias=0.0, scale=-0.5))

                rstd_ac = s_ac.v

                @blk.vector
                def _(v, par=par, st_=st_):
                    allwait(v)
                    v.wait_ge(s_ac.h, rstd_ac)
                    v.wait_ge(s_pe.h, s_pe.v)
                    s_dv.inc(nc.vector.tensor_tensor(
                        sqbuf[:], xtile[par], st_[:, 2:3].broadcast_to([128, 1024]),
                        op=OP.subtract))
                    v.wait_ge(s_dv.h, s_dv.v)
                    s_dv.inc(nc.vector.tensor_tensor(
                        xnbuf[:], sqbuf[:], st_[:, 7:8].broadcast_to([128, 1024]),
                        op=OP.mult))

                xn_dv = s_dv.v

                @blk.tensor
                def _(t, par=par):
                    allwait(t)
                    t.wait_ge(s_dv.h, xn_dv)
                    t.wait_ge(s_dv.h, s_dv.v)
                    for fc in range(8):
                        instr = nc.tensor.transpose(psT[:, fc * 128:(fc + 1) * 128],
                                                    xnbuf[:, fc * 128:(fc + 1) * 128], identb[:])
                    for g in range(4):
                        instr = nc.tensor.transpose(psX[:, g * 128:(g + 1) * 128],
                                                    xtile[par][:, g * 128:(g + 1) * 128], identf[:])
                    s_pe.inc(instr)

                pe_done.append(s_pe.v)
                tp_pe = s_pe.v

                if debug and i < 8:
                    dbg_xnb = nc.dram_tensor(f"dbg_xnb{i}", [128, 1024], BF16,
                                             kind="ExternalOutput").ap()
                    dbg_stats = nc.dram_tensor(f"dbg_stats{i}", [128, 12], F32,
                                               kind="ExternalOutput").ap()

                    @blk.sync
                    def _(q, i=i, st_=st_):
                        q.wait_ge(s_pe.h, tp_pe)
                        s_st.dma(q.dma_start(out=dbg_xnb, in_=xnbuf[:]))
                        s_st.dma(q.dma_start(out=dbg_stats, in_=st_))

                    st_snap = s_st.v

                @blk.vector
                def _(v, i=i):
                    allwait(v)
                    if debug and i < 8:
                        v.wait_ge(s_st.h, st_snap)
                    v.wait_ge(s_pe.h, tp_pe)
                    if i >= 1:
                        v.wait_ge(s_st.h, s_st.v)
                    nc.vector.tensor_copy(
                        xn_lo[:, :, i * 128:(i + 1) * 128],
                        psT[:, 0:512].rearrange("p (f t) -> p f t", t=128))
                    nc.vector.tensor_copy(xstage[:], psT[:, 512:1024])
                    s_dv.inc(nc.vector.tensor_copy(
                        oa[:, :, i * 128:(i + 1) * 128],
                        psX[:].rearrange("p (g t) -> p g t", t=128)))

                ev_dv = s_dv.v

                @blk.sync
                def _(q, i=i):
                    q.wait_ge(s_dv.h, ev_dv)
                    s_st.dma(q.dma_start(
                        out=xnup_dv[:, :, i * 128:(i + 1) * 128],
                        in_=xstage[:].rearrange("p (f t) -> p f t", t=128)))

    try:
        _ps.close()
    except NameError:
        pass

    with nc.Block() as blk:
        @blk.gpsimd
        def _(g):
            for dst, src in [
                (wb_bf[:], wb_d), (wc_bf[:], wc_d),
                (wg_bf[:], wg_d), (bexp_bf[:], bexp_d),
            ]:
                s_gld.dma(g.dma_start(out=dst, in_=src))

    wcast_gld = s_gld.v
    xnup_done = s_st.v

    if debug:
        with nc.Block() as blk:
            dbg_xnlow = nc.dram_tensor("dbg_xnlow", [128, 4 * S], BF16, kind="ExternalOutput").ap()
            dbg_oa1 = nc.dram_tensor("dbg_oa1", [128, 4 * S], F32, kind="ExternalOutput").ap()

            @blk.sync
            def _(q):
                q.wait_ge(s_dv.h, s_dv.v)
                s_st.dma(q.dma_start(out=dbg_xnlow, in_=xnlow[:, 0:4 * S]))
                s_st.dma(q.dma_start(out=dbg_oa1, in_=out_acc[:]))
                q.wait_ge(s_st.h, s_st.v)
    xnup_done = s_st.v

    if "B" in phases:
        with nc.Block() as blk:
            xup = [kview(0, (128, 4, 512), BF16), kview(4096, (128, 4, 512), BF16)]
            zstage = kview(8192, (8, 512), F32)
            zt_sb = kview(10240, (128, 4, 8), F32)
            combT = kview(10368, (128, 4, 8), F32)
            cstage = kview(10496, (8, 512), F32)
            bstage = kview(12544, (16, 512), BF16)
            cstage2 = kview(14592, (16, 512), BF16)
            gtmp = kview(16640, (128, 4, 16), F32)
            egbuf = kview(16896 + 128, (128, 4, 8), F32)
            _psB = contextlib.ExitStack()
            psZ = _psB.enter_context(nc.psum_tensor("ps_auto3", [16, 512], F32))[:]
            psZT = _psB.enter_context(nc.psum_tensor("ps_auto4", [128, 32], F32))[:]
            psBC = _psB.enter_context(nc.psum_tensor("ps_auto5", [16, 1024], F32))[:]
            psCT = _psB.enter_context(nc.psum_tensor("ps_auto6", [8, 512], F32))[:]
            pe_done = []

            for tc in range(NTC):
                par = tc % 2

                @blk.gpsimd
                def _(g, tc=tc, par=par):
                    if tc == 0:
                        g.wait_ge(s_st.h, xnup_done)
                    if tc >= 2:
                        g.wait_ge(s_pe.h, pe_done[3 * (tc - 2)])
                    s_gld.dma(g.dma_start(out=xup[par],
                                          in_=xnup_dv[:, :, tc * 512:(tc + 1) * 512]))

                ld_here = s_gld.v

                @blk.tensor
                def _(t, tc=tc, par=par):
                    t.wait_ge(s_gld.h, ld_here)
                    if tc == 0:
                        t.wait_ge(s_ld.h, s_ld.v)
                    t.wait_ge(s_dv.h, s_dv.v)
                    wgv = wg_bf[:].rearrange("p (f e) -> p f e", e=NE)
                    wbv = wb_bf[:].rearrange("p (f n) -> p f n", n=NST)
                    wcv = wc_bf[:].rearrange("p (f n) -> p f n", n=NST)
                    for fc in range(8):
                        rhs = xn_lo[:, fc, tc * 512:(tc + 1) * 512] if fc < 4 else xup[par][:, fc - 4, :]
                        nc.tensor.matmul(psZ[0:8, :], wgv[:, fc, :], rhs,
                                         start=(fc == 0), stop=(fc == 7))
                        nc.tensor.matmul(psBC[:, 0:512], wbv[:, fc, :], rhs,
                                         start=(fc == 0), stop=(fc == 7))
                        instr = nc.tensor.matmul(psBC[:, 512:1024], wcv[:, fc, :], rhs,
                                                 start=(fc == 0), stop=(fc == 7))
                    s_pe.inc(instr)

                pe_done.append(s_pe.v)
                mm_pe = s_pe.v

                @blk.vector
                def _(v, tc=tc):
                    v.wait_ge(s_pe.h, mm_pe)
                    if tc >= 1:
                        v.wait_ge(s_st.h, s_st.v)
                    nc.vector.tensor_scalar(zstage[:], psZ[0:8, :], bgv[:], None, op0=OP.add)
                    nc.vector.tensor_scalar(bstage[:], psBC[:, 0:512], bbv[:], None, op0=OP.add)
                    s_dv.inc(nc.vector.tensor_scalar(cstage2[:], psBC[:, 512:1024], bcv[:], None, op0=OP.add))

                ev_dv = s_dv.v

                @blk.sync
                def _(q, tc=tc):
                    q.wait_ge(s_dv.h, ev_dv)
                    s_st.dma(q.dma_start(out=bT_d[:, tc * 512:(tc + 1) * 512], in_=bstage[:]))
                    s_st.dma(q.dma_start(out=cT_d[:, tc * 512:(tc + 1) * 512], in_=cstage2[:]))

                if "3" in bparts:
                    @blk.vector
                    def _(v, tc=tc):
                        for g in range(4):
                            s_dv.inc(nc.vector.scalar_tensor_tensor(
                                oa[:, g, tc * 512:(tc + 1) * 512],
                                xn_lo[:, g, tc * 512:(tc + 1) * 512], dskip[:, g:g + 1],
                                oa[:, g, tc * 512:(tc + 1) * 512],
                                op0=OP.mult, op1=OP.add))

                if "2" not in bparts:
                    pe_done.append(s_pe.v)
                    pe_done.append(s_pe.v)
                    continue

                @blk.tensor
                def _(t, tc=tc):
                    t.wait_ge(s_dv.h, ev_dv)
                    for j in range(4):
                        instr = nc.tensor.transpose(psZT[:, j * 8:(j + 1) * 8],
                                                    zstage[0:8, j * 128:(j + 1) * 128],
                                                    identf[0:8, 0:8])
                    s_pe.inc(instr)

                pe_done.append(s_pe.v)
                zt_pe = s_pe.v

                @blk.vector
                def _(v, tc=tc):
                    v.wait_ge(s_pe.h, zt_pe)
                    s_dv.inc(nc.vector.tensor_copy(zt_sb[:], psZT[:].rearrange("p (j e) -> p j e", e=8)))
                    v.wait_ge(s_dv.h, s_dv.v)
                    for j in range(4):
                        s_dv.inc(nc.vector.tensor_reduce(
                            gtmp[:, j, 0:1], zt_sb[:, j, :], axis=AX.X, op=OP.max, negate=True))

                nmax_dv = s_dv.v

                @blk.vector
                def _(v, tc=tc):
                    v.wait_ge(s_dv.h, nmax_dv)
                    for j in range(4):
                        s_dv.inc(nc.vector.tensor_scalar(
                            egbuf[:, j, :], zt_sb[:, j, :], gtmp[:, j, 0:1], None,
                            op0=OP.add))

                zms_dv = s_dv.v

                @blk.scalar
                def _(a, tc=tc, zms_dv=zms_dv):
                    a.wait_ge(s_dv.h, zms_dv)
                    for j in range(4):
                        s_ac.inc(nc.scalar.activation(
                            egbuf[:, j, :], egbuf[:, j, :], AF.Exp,
                            bias=0.0, scale=1.0, accum_out=gtmp[:, j, 1:2]))

                eg_ac = s_ac.v

                @blk.vector
                def _(v, tc=tc):
                    v.wait_ge(s_ac.h, eg_ac)
                    for j in range(4):
                        nc.vector.reciprocal(gtmp[:, j, 2:3], gtmp[:, j, 1:2])
                        s_dv.inc(nc.vector.tensor_scalar(
                            gtmp[:, j, 8:16], zt_sb[:, j, :], gtmp[:, j, 0:1], 0.0,
                            op0=OP.add, op1=OP.is_ge))
                        v.wait_ge(s_dv.h, s_dv.v)
                        s_dv.inc(nc.vector.scalar_tensor_tensor(
                            gtmp[:, j, 8:16], gtmp[:, j, 8:16], -1e30, zt_sb[:, j, :],
                            op0=OP.mult, op1=OP.add))
                        v.wait_ge(s_dv.h, s_dv.v)
                        s_dv.inc(nc.vector.tensor_reduce(
                            gtmp[:, j, 3:4], gtmp[:, j, 8:16], axis=AX.X, op=OP.max, negate=True))
                        v.wait_ge(s_dv.h, s_dv.v)
                        s_dv.inc(nc.vector.tensor_scalar(
                            gtmp[:, j, 8:16], zt_sb[:, j, :], gtmp[:, j, 3:4], 0.0,
                            op0=OP.add, op1=OP.is_ge))
                        v.wait_ge(s_dv.h, s_dv.v)
                        s_dv.inc(nc.vector.scalar_tensor_tensor(
                            combT[:, j, :], egbuf[:, j, :], gtmp[:, j, 2:3], gtmp[:, j, 8:16],
                            op0=OP.mult, op1=OP.mult))

                comb_dv = s_dv.v

                @blk.tensor
                def _(t, tc=tc):
                    t.wait_ge(s_dv.h, comb_dv)
                    for j in range(4):
                        instr = nc.tensor.transpose(psCT[:, j * 128:(j + 1) * 128],
                                                    combT[:, j, :], identf[:])
                    s_pe.inc(instr)

                pe_done.append(s_pe.v)
                ct_pe = s_pe.v

                @blk.vector
                def _(v, tc=tc):
                    v.wait_ge(s_pe.h, ct_pe)
                    s_dv.inc(nc.vector.tensor_copy(cstage[:], psCT[0:8, :]))

                cs_dv = s_dv.v

                @blk.sync
                def _(q, tc=tc):
                    q.wait_ge(s_dv.h, cs_dv)
                    s_st.dma(q.dma_start(out=comb_d[:, tc * 512:(tc + 1) * 512], in_=cstage[:]))


    try:
        _psB.close()
    except NameError:
        pass
    comb_done = s_st.v

    if "C" in phases:
        with nc.Block() as blk:
            xup = [kview(0, (128, 4, 512), BF16), kview(4096, (128, 4, 512), BF16)]
            xebuf = [vview(0, (128, 8, 512), BF16), vview(8192, (128, 8, 512), BF16)]
            cbc = [vview(16384, (128, 512), BF16), vview(17408, (128, 512), BF16)]
            _psC = contextlib.ExitStack()
            psEx = _psC.enter_context(nc.psum_tensor("ps_auto7", [128, 2048], F32))[:]
            mm_done = []
            ev_done = []
            bias_done = []

            for tc in range(NTC):
                par = tc % 2

                @blk.gpsimd
                def _(g, tc=tc, par=par):
                    if tc == 0:
                        g.wait_ge(s_st.h, comb_done)
                    if tc >= 2:
                        g.wait_ge(s_pe.h, mm_done[NE * (tc - 2) + NE - 1])
                        g.wait_ge(s_pe.h, bias_done[tc - 2])
                    s_gld.dma(g.dma_start(out=xup[par], in_=xnup_dv[:, :, tc * 512:(tc + 1) * 512]))
                    s_gld.dma(g.dma_start(out=cbrow[par][:], in_=comb_d[:, tc * 512:(tc + 1) * 512]))

                for e in range(NE):
                    epar = e % 2
                    ge = tc * NE + e

                    @blk.gpsimd
                    def _(g, tc=tc, e=e, epar=epar, ge=ge):
                        if ge >= 2:
                            g.wait_ge(s_pe.h, mm_done[ge - 2])
                        s_gld.dma(g.dma_start(out=wexp_bf[epar][:], in_=wexp_d[e]))
                        src = comb_d[e:e + 1, tc * 512:(tc + 1) * 512].partition_broadcast(128)
                        s_gld.dma(g.dma_start(out=cbc[epar][:], in_=src))

                    eld = s_gld.v

                    @blk.vector
                    def _(v, tc=tc, e=e, epar=epar, par=par, ge=ge):
                        v.wait_ge(s_gld.h, eld)
                        if ge >= 2:
                            v.wait_ge(s_pe.h, mm_done[ge - 2])
                        xe = xebuf[epar]
                        nc.vector.tensor_tensor(
                            xe[:, 0:4, :],
                            xn_lo[:, :, tc * 512:(tc + 1) * 512],
                            cbc[epar][:].unsqueeze(1).broadcast_to([128, 4, 512]),
                            op=OP.mult)
                        s_dv.inc(nc.vector.tensor_tensor(
                            xe[:, 4:8, :], xup[par],
                            cbc[epar][:].unsqueeze(1).broadcast_to([128, 4, 512]),
                            op=OP.mult))

                    xe_dv = s_dv.v

                    @blk.tensor
                    def _(t, tc=tc, e=e, epar=epar):
                        t.wait_ge(s_dv.h, xe_dv)
                        if e == 0 and tc > 0:
                            t.wait_ge(s_dv.h, ev_done[tc - 1])
                        wx = wexp_bf[epar][:].rearrange("p (f g q) -> p f g q", g=4, q=128)
                        for g in range(4):
                            for fc in range(8):
                                instr = nc.tensor.matmul(
                                    psEx[:, g * 512:(g + 1) * 512],
                                    wx[:, fc, g, :], xebuf[epar][:, fc, :],
                                    start=(e == 0 and fc == 0), stop=False)
                        s_pe.inc(instr)

                    mm_done.append(s_pe.v)

                @blk.tensor
                def _(t, tc=tc, par=par):
                    for g in range(4):
                        instr = nc.tensor.matmul(
                            psEx[:, g * 512:(g + 1) * 512],
                            bexp_bf[:, g * 128:(g + 1) * 128], cbrow[par][:],
                            start=False, stop=True)
                    s_pe.inc(instr)

                bias_done.append(s_pe.v)
                ex_pe = s_pe.v

                @blk.vector
                def _(v, tc=tc, ex_pe=ex_pe):
                    v.wait_ge(s_pe.h, ex_pe)
                    for g in range(4):
                        s_dv.inc(nc.vector.scalar_tensor_tensor(
                            oa[:, g, tc * 512:(tc + 1) * 512],
                            psEx[:, g * 512:(g + 1) * 512], 1.0,
                            oa[:, g, tc * 512:(tc + 1) * 512],
                            op0=OP.mult, op1=OP.add))

                ev_done.append(s_dv.v)

    try:
        _psC.close()
    except NameError:
        pass

    if "D" in phases:
        with nc.Block() as blk:
            TB = 512
            NTB = S // TB
            dt_sb = kview(0, (128, TB), BF16)
            e1_sb = kview(2048, (128, TB), F32)
            w_sb = kview(4096, (128, TB), BF16)
            a_sb = [kview(6144, (128, TB), F32), kview(8192, (128, TB), F32)]
            u_sb = [kview(10240, (128, TB), F32), kview(12288, (128, TB), F32)]
            h_sb = [kview(14336, (128, TB), F32), kview(16384, (128, TB), F32)]
            hc_sb = [kview(18432, (128, TB), BF16), kview(19456, (128, TB), BF16)]
            brep = [kview(20480, (128, TB), BF16), kview(21504, (128, TB), BF16)]
            crep = [kview(22528, (128, TB), BF16), kview(23552, (128, TB), BF16)]
            xupd = [kview(24576, (128, 4, TB), BF16), kview(28672, (128, 4, TB), BF16)]
            r1b = kview(30720, (128, 512), BF16)

            zred = _mkview(wexp_bf[0], 0, (128, 2048), BF16)
            r1f = _mkview(wexp_bf[1], 0, (128, 2048), BF16)

            @blk.gpsimd
            def _(g):
                s_gld.dma(g.dma_start(out=wbig[:], in_=wd_d))
                s_gld.dma(g.dma_start(out=zred, in_=zred_d))
                s_gld.dma(g.dma_start(out=r1f, in_=r1f_d))

            wdld = s_gld.v
            _psD = contextlib.ExitStack()
            psDt = _psD.enter_context(nc.psum_tensor("ps_auto8", [128, 512], F32))[:]
            psR = [_psD.enter_context(nc.psum_tensor(f"ps_psR{i}", [128, 1024], F32))[:] for i in range(2)]
            psY = _psD.enter_context(nc.psum_tensor("ps_auto10", [128, 512], F32))[:]
            prev_evac_dv = None
            it_end_dv = []
            it_end_po = []

            for g4 in range(4):
                for tb in range(NTB):
                    it = g4 * NTB + tb
                    par = it % 2

                    pe0 = s_pe.v
                    ac0 = s_ac.v
                    dv0 = s_dv.v
                    po0 = s_po.v

                    pe_wd = pe0 + 1
                    pe_repl = {}
                    pe_z = {}
                    c = pe_wd
                    order = []
                    for k in range(2):
                        c += 1; pe_repl[k] = c; order.append(("r", k))
                    for k in range(16):
                        c += 1; pe_z[k] = c; order.append(("z", k))
                        if k + 2 < 16:
                            c += 1; pe_repl[k + 2] = c; order.append(("r", k + 2))
                    pe_end = c
                    ac_sp = ac0 + 2
                    ac_exp = {k: ac_sp + k + 1 for k in range(16)}
                    dv_w = dv0 + 1
                    dv_u = {k: dv_w + 3 * k + 1 for k in range(16)}
                    dv_scan = {k: dv_w + 3 * k + 2 for k in range(16)}
                    dv_state = {k: dv_w + 3 * k + 3 for k in range(16)}
                    dv_evac = dv_w + 48 + 1
                    po_hc = {k: po0 + k + 1 for k in range(16)}

                    @blk.sync
                    def _(q, g4=g4, tb=tb, par=par, it=it):
                        if it == 0:
                            q.wait_ge(s_st.h, comb_done)
                        if it >= 2:
                            q.wait_ge(s_dv.h, it_end_dv[it - 2])
                            q.wait_ge(s_po.h, it_end_po[it - 2])
                        s_ld.dma(q.dma_start(out=xupd[par],
                                             in_=xnup_dv[:, :, tb * TB:(tb + 1) * TB]))
                        bsrc = bT_d[:, tb * TB:(tb + 1) * TB].unsqueeze(0).broadcast_to([8, NST, TB])
                        csrc = cT_d[:, tb * TB:(tb + 1) * TB].unsqueeze(0).broadcast_to([8, NST, TB])
                        s_ld.dma(q.dma_start(out=brep[par][:], in_=bsrc))
                        s_ld.dma(q.dma_start(out=crep[par][:], in_=csrc))

                    ld_here = s_ld.v

                    @blk.gpsimd
                    def _(gq, g4=g4, tb=tb, par=par, it=it, ld_here=ld_here):
                        gq.wait_ge(s_ld.h, ld_here)
                        for k in range(16):
                            kp = k % 2
                            gq.wait_ge(s_dv.h, dv_scan[k])
                            if k >= 2:
                                gq.wait_ge(s_pe.h, pe_z[k - 2])
                            elif it >= 1:
                                gq.wait_ge(s_pe.h, pe0)
                            s_po.inc(nc.gpsimd.tensor_tensor(
                                hc_sb[kp][:], h_sb[kp][:], crep[par][:], op=OP.mult))

                    @blk.tensor
                    def _(t, g4=g4, tb=tb, par=par, it=it, ld_here=ld_here):
                        t.wait_ge(s_ld.h, ld_here)
                        if it == 0:
                            t.wait_ge(s_gld.h, s_gld.v)
                        t.wait_ge(s_ac.h, ac0)
                        if prev_evac_dv is not None:
                            t.wait_ge(s_dv.h, prev_evac_dv)
                        wdv = wbig[:].rearrange("p (f c) -> p f c", c=DH)
                        for fc in range(8):
                            rhs = (xn_lo[:, fc, tb * TB:(tb + 1) * TB] if fc < 4
                                   else xupd[par][:, fc - 4, :])
                            instr = nc.tensor.matmul(
                                psDt[:], wdv[:, fc, g4 * 128:(g4 + 1) * 128], rhs,
                                start=(fc == 0), stop=(fc == 7))
                        s_pe.inc(instr)
                        for kind, k in order:
                            kp = k % 2
                            r, m = k // 4, k % 4
                            if kind == "r":
                                t.wait_ge(s_dv.h, dv_w)
                                if k >= 2:
                                    t.wait_ge(s_ac.h, ac_exp[k - 2])
                                    t.wait_ge(s_dv.h, dv_u[k - 2])
                                nc.tensor.matmul(psR[kp][:, 0:512],
                                                 r1f[:, k * 128:(k + 1) * 128],
                                                 dt_sb[:], start=True, stop=True)
                                instr = nc.tensor.matmul(psR[kp][:, 512:1024],
                                                         r1f[:, k * 128:(k + 1) * 128],
                                                         w_sb[:], start=True, stop=True)
                                s_pe.inc(instr)
                            else:
                                t.wait_ge(s_po.h, po_hc[k])
                                instr = nc.tensor.matmul(
                                    psY[:], zred[:, k * 128:(k + 1) * 128], hc_sb[kp][:],
                                    start=(k == 0), stop=(k == 15))
                                s_pe.inc(instr)

                    assert s_pe.v == pe_end, (s_pe.v, pe_end)

                    @blk.scalar
                    def _(a, g4=g4, it=it):
                        a.wait_ge(s_pe.h, pe_wd)
                        if prev_evac_dv is not None:
                            a.wait_ge(s_dv.h, dv0)
                        s_ac.inc(nc.scalar.activation(e1_sb[:], psDt[:], AF.Exp,
                                                      bias=bd[:, g4:g4 + 1], scale=1.0))
                        s_ac.inc(nc.scalar.activation(dt_sb[:], e1_sb[:], AF.Ln, bias=1.0, scale=1.0))
                        for k in range(16):
                            kglob = g4 * 16 + k
                            a.wait_ge(s_pe.h, pe_repl[k])
                            if k >= 2:
                                a.wait_ge(s_dv.h, dv_scan[k - 2])
                            elif it >= 1:
                                a.wait_ge(s_dv.h, dv0)
                            s_ac.inc(nc.scalar.activation(
                                a_sb[k % 2][:], psR[k % 2][:, 0:512], AF.Exp,
                                bias=0.0, scale=a_il[:, kglob:kglob + 1]))

                    assert s_ac.v == ac_exp[15], (s_ac.v, ac_exp[15])

                    @blk.vector
                    def _(v, g4=g4, tb=tb, par=par, it=it):
                        v.wait_ge(s_ac.h, ac_sp)
                        s_dv.inc(nc.vector.tensor_tensor(
                            w_sb[:], dt_sb[:], xn_lo[:, g4, tb * TB:(tb + 1) * TB], op=OP.mult))
                        for k in range(16):
                            kp = k % 2
                            kglob = g4 * 16 + k
                            v.wait_ge(s_pe.h, pe_repl[k])
                            if k >= 2:
                                v.wait_ge(s_po.h, po_hc[k - 2])
                            elif it >= 1:
                                v.wait_ge(s_po.h, po0)
                            s_dv.inc(nc.vector.tensor_tensor(
                                u_sb[kp][:], psR[kp][:, 512:1024], brep[par][:], op=OP.mult))
                            v.wait_ge(s_ac.h, ac_exp[k])
                            v.wait_ge(s_dv.h, dv_u[k])
                            init = 0.0 if tb == 0 else state[:, kglob:kglob + 1]
                            s_dv.inc(nc.vector.tensor_tensor_scan(
                                h_sb[kp][:], a_sb[kp][:], u_sb[kp][:], init,
                                op0=OP.mult, op1=OP.add))
                            v.wait_ge(s_dv.h, dv_scan[k])
                            s_dv.inc(nc.vector.tensor_copy(state[:, kglob:kglob + 1],
                                                           h_sb[kp][:, TB - 1:TB]))
                        v.wait_ge(s_pe.h, pe_end)
                        s_dv.inc(nc.vector.scalar_tensor_tensor(
                            oa[:, g4, tb * TB:(tb + 1) * TB], psY[:], 1.0,
                            oa[:, g4, tb * TB:(tb + 1) * TB], op0=OP.mult, op1=OP.add))

                    assert s_dv.v == dv_evac, (s_dv.v, dv_evac)
                    prev_evac_dv = dv_evac
                    it_end_dv.append(s_dv.v)
                    it_end_po.append(s_po.v)

    try:
        _psD.close()
    except NameError:
        pass

    if "E" in phases:
        with nc.Block() as blk:
            xup = [_mkview(wexp_bf[0], 0, (128, 4, 512), BF16),
                   _mkview(wexp_bf[0], 4096, (128, 4, 512), BF16)]
            vstage = _mkview(wexp_bf[1], 0, (128, 4, 512), BF16)
            qstage = _mkview(wexp_bf[1], 4096, (128, 4, 512), BF16)
            _psE = contextlib.ExitStack()
            psP = _psE.enter_context(nc.psum_tensor("ps_autoP", [128, 2048], F32))[:]
            psVT = [_psE.enter_context(nc.psum_tensor(f"ps_psVT{i}", [128, 512], BF16))[:] for i in range(2)]

            @blk.gpsimd
            def _(g):
                s_po.inc(nc.gpsimd.memset(vt_h[:, :, :, 64:65], 1.0))

            ones_po = s_po.v
            proj_pe = []

            for di, dst in enumerate(("k", "v", "q")):
                w_src = {"k": wk_d, "v": wv_d, "q": wq_d}[dst]

                @blk.gpsimd
                def _(g, di=di, dst=dst, w_src=w_src):
                    if di > 0:
                        g.wait_ge(s_pe.h, proj_pe[-1])
                    s_gld.dma(g.dma_start(out=wbig[:], in_=w_src))

                wld = s_gld.v

                for tc in range(NTC):
                    par = tc % 2

                    @blk.gpsimd
                    def _(g, tc=tc, par=par):
                        if len(proj_pe) >= 2:
                            g.wait_ge(s_pe.h, proj_pe[-2])
                        s_gld.dma(g.dma_start(out=xup[par],
                                              in_=xnup_dv[:, :, tc * 512:(tc + 1) * 512]))

                    ld_here = s_gld.v

                    @blk.tensor
                    def _(t, tc=tc, par=par, wld=wld, ld_here=ld_here):
                        t.wait_ge(s_gld.h, max(wld, ld_here))
                        t.wait_ge(s_dv.h, s_dv.v)
                        w = wbig[:].rearrange("p (f c) -> p f c", c=DH)
                        for pg in range(4):
                            for fc in range(8):
                                rhs = (xn_lo[:, fc, tc * 512:(tc + 1) * 512] if fc < 4
                                       else xup[par][:, fc - 4, :])
                                instr = nc.tensor.matmul(
                                    psP[:, pg * 512:(pg + 1) * 512],
                                    w[:, fc, pg * 128:(pg + 1) * 128], rhs,
                                    start=(fc == 0), stop=(fc == 7))
                        s_pe.inc(instr)

                    proj_pe.append(s_pe.v)
                    mm_pe = s_pe.v

                    if dst == "k":
                        @blk.vector
                        def _(v, tc=tc, mm_pe=mm_pe):
                            v.wait_ge(s_pe.h, mm_pe)
                            for pg in range(4):
                                s_dv.inc(nc.vector.tensor_scalar(
                                    KT_v[:, pg, tc * 512:(tc + 1) * 512],
                                    psP[:, pg * 512:(pg + 1) * 512],
                                    bk[:, pg:pg + 1], None, op0=OP.add))
                    elif dst == "v":
                        @blk.vector
                        def _(v, tc=tc, mm_pe=mm_pe):
                            v.wait_ge(s_pe.h, mm_pe)
                            for pg in range(4):
                                s_dv.inc(nc.vector.tensor_scalar(
                                    vstage[:, pg, :], psP[:, pg * 512:(pg + 1) * 512],
                                    bv[:, pg:pg + 1], None, op0=OP.add))

                        vs_dv = s_dv.v

                        for pg in range(4):
                            @blk.tensor
                            def _(t, tc=tc, pg=pg, vs_dv=vs_dv):
                                t.wait_ge(s_dv.h, s_dv.v)
                                if tc == 0 and pg == 0:
                                    t.wait_ge(s_po.h, ones_po)
                                for j in range(4):
                                    instr = nc.tensor.transpose(
                                        psVT[pg % 2][:, j * 128:(j + 1) * 128],
                                        vstage[:, pg, :][:, j * 128:(j + 1) * 128], identb[:])
                                s_pe.inc(instr)

                            vt_pe = s_pe.v

                            @blk.vector
                            def _(v, tc=tc, pg=pg, vt_pe=vt_pe):
                                v.wait_ge(s_pe.h, vt_pe)
                                s_dv.inc(nc.vector.tensor_copy(
                                    vt_h[:, tc * 4:(tc + 1) * 4, 2 * pg:2 * pg + 2, 0:64],
                                    psVT[pg % 2][:].rearrange("p (a h j) -> p a h j", h=2, j=64)))
                    else:
                        @blk.vector
                        def _(v, tc=tc, mm_pe=mm_pe):
                            v.wait_ge(s_pe.h, mm_pe)
                            if tc >= 1:
                                v.wait_ge(s_st.h, s_st.v)
                            for pg in range(4):
                                s_dv.inc(nc.vector.tensor_scalar(
                                    qstage[:, pg, :], psP[:, pg * 512:(pg + 1) * 512],
                                    bq[:, pg:pg + 1], None, op0=OP.add))

                        q_dv = s_dv.v

                        @blk.sync
                        def _(q, tc=tc, q_dv=q_dv):
                            q.wait_ge(s_dv.h, q_dv)
                            s_st.dma(q.dma_start(
                                out=qt_dv[:, :, tc * 512:(tc + 1) * 512], in_=qstage[:]))

    try:
        _psE.close()
    except NameError:
        pass
    qt_done = s_st.v

    if "F" in phases:
        with nc.Block() as blk:
            NKT = S // 128
            NQC = S // 512
            pT = [[xview(0, (128, 1024), BF16), xview(2048, (128, 1024), BF16)],
                  [xview(4096, (128, 1024), BF16), xview(6144, (128, 1024), BF16)]]
            attnU = xview(8192, (128, NQC * 512), BF16)
            tmpn = xview(8192 + NQC * 1024, (128, 512), F32)
            stage64 = xview(8192 + NQC * 1024 + 2048, (128, 1024), F32)
            dtile = xview(8192 + NQC * 1024 + 6144, (16, 512), F32)
            rdtile = xview(8192 + NQC * 1024 + 8192, (16, 512), F32)
            tmpb = xview(8192 + NQC * 1024 + 10240, (128, 512), BF16)
            _psF = contextlib.ExitStack()
            psS = [_psF.enter_context(nc.psum_tensor(f"ps_psS{i}", [128, 512], F32))[:] for i in range(4)]
            psO = [_psF.enter_context(nc.psum_tensor(f"ps_psO{i}", [128, 512], F32))[:] for i in range(2)]
            psD = _psF.enter_context(nc.psum_tensor("ps_auto15", [128, 512], F32))[:]
            exp_ac = {}
            pv_pe = {}
            nrm_dv = [0]
            nrm_pe = {}

            for pr in range(4):
                @blk.sync
                def _(q, pr=pr):
                    if pr == 0:
                        q.wait_ge(s_st.h, qt_done)
                    if pr >= 1:
                        q.wait_ge(s_dv.h, nrm_dv[-1])
                    s_st.dma(q.dma_start(out=qt_pair[:], in_=qt_dv[:, pr, :]))

                qld = s_st.v

                den_dma = []
                evac_dv = []
                for qc in range(NQC):
                    pe0 = s_pe.v
                    ac0 = s_ac.v
                    pe_sc = {}
                    pe_pv = {}
                    c = pe0
                    order = []
                    for kt in range(2):
                        for hh in range(2):
                            c += 1; pe_sc[(kt, hh)] = c; order.append(("s", kt, hh))
                    for kt in range(NKT):
                        for hh in range(2):
                            c += 1; pe_pv[(kt, hh)] = c; order.append(("p", kt, hh))
                        if kt + 2 < NKT:
                            for hh in range(2):
                                c += 1; pe_sc[(kt + 2, hh)] = c; order.append(("s", kt + 2, hh))
                    pe_end = c
                    ac_exp = {}
                    c = ac0
                    for kt in range(NKT):
                        for hh in range(2):
                            c += 1; ac_exp[(kt, hh)] = c

                    @blk.tensor
                    def _(t, pr=pr, qc=qc, order=order, pe_sc=pe_sc, ac_exp=ac_exp,
                          qld=qld, evac_dv=tuple(evac_dv)):
                        if qc == 0:
                            t.wait_ge(s_st.h, qld)
                        for kind, kt, hh in order:
                            kp = kt % 2
                            if kind == "s":
                                if kt >= 2:
                                    t.wait_ge(s_ac.h, ac_exp[(kt - 2, hh)])
                                elif qc >= 1:
                                    t.wait_ge(s_ac.h, ac0)
                                instr = nc.tensor.matmul(
                                    psS[2 * (kt % 2) + hh][:, 0:512],
                                    KT_v[64 * hh:64 * (hh + 1), pr, kt * 128:(kt + 1) * 128],
                                    qt_pair[64 * hh:64 * (hh + 1), qc * 512:(qc + 1) * 512],
                                    start=True, stop=True, tile_position=(64 * hh, 0))
                                s_pe.inc(instr)
                            else:
                                t.wait_ge(s_ac.h, ac_exp[(kt, hh)])
                                if kt == 0 and qc >= 1:
                                    t.wait_ge(s_dv.h, evac_dv[-1])
                                h_core = 2 * pr + hh
                                instr = nc.tensor.matmul(
                                    psO[hh][0:65, :],
                                    vt_v[:, kt, 65 * h_core:65 * h_core + 65],
                                    pT[hh][kt % 2][:, 0:512],
                                    start=(kt == 0), stop=(kt == NKT - 1))
                                s_pe.inc(instr)

                    assert s_pe.v == pe_end

                    @blk.scalar
                    def _(a, pr=pr, qc=qc, pe_sc=pe_sc, pe_pv=pe_pv):
                        for kt in range(NKT):
                            for hh in range(2):
                                a.wait_ge(s_pe.h, pe_sc[(kt, hh)])
                                if kt >= 2:
                                    a.wait_ge(s_pe.h, pe_pv[(kt - 2, hh)])
                                s_ac.inc(nc.scalar.activation(
                                    pT[hh][kt % 2][:, 0:512], psS[2 * (kt % 2) + hh][:, 0:512],
                                    AF.Exp, bias=-4.0, scale=0.125))

                    assert s_ac.v == ac_exp[(NKT - 1, 1)]

                    o_pe = s_pe.v

                    @blk.vector
                    def _(v, pr=pr, qc=qc, o_pe=o_pe):
                        v.wait_ge(s_pe.h, o_pe)
                        if qc >= 1:
                            v.wait_ge(s_st.h, den_dma[-1])
                        nc.vector.tensor_copy(attnU[0:64, qc * 512:(qc + 1) * 512], psO[0][0:64, :])
                        nc.vector.tensor_copy(tmpb[0:64, :], psO[1][0:64, :])
                        nc.vector.tensor_copy(stage64[64:65, 0:512], psO[0][64:65, :])
                        s_dv.inc(nc.vector.tensor_copy(stage64[64:65, 512:1024], psO[1][64:65, :]))

                    st_dv = s_dv.v

                    @blk.tensor
                    def _(t, qc=qc, st_dv=st_dv):
                        t.wait_ge(s_dv.h, st_dv)
                        s_pe.inc(nc.tensor.matmul(
                            psD[64:128, :], identb[0:64, 0:64], tmpb[0:64, :],
                            start=True, stop=True, tile_position=(0, 64)))

                    sh_pe = s_pe.v

                    @blk.vector
                    def _(v, qc=qc, sh_pe=sh_pe):
                        v.wait_ge(s_pe.h, sh_pe)
                        s_dv.inc(nc.vector.tensor_copy(
                            attnU[64:128, qc * 512:(qc + 1) * 512], psD[64:128, :]))

                    evac_dv.append(s_dv.v)

                    @blk.sync
                    def _(q, pr=pr, qc=qc, st_dv=st_dv):
                        q.wait_ge(s_dv.h, st_dv)
                        s_st.dma(q.dma_start(
                            out=den_d[2 * pr * 8 + qc:2 * pr * 8 + qc + 1, :],
                            in_=stage64[64:65, 0:512]))
                        s_st.dma(q.dma_start(
                            out=den_d[(2 * pr + 1) * 8 + qc:(2 * pr + 1) * 8 + qc + 1, :],
                            in_=stage64[64:65, 512:1024]))

                    den_dma.append(s_st.v)

                @blk.sync
                def _(q, pr=pr):
                    q.wait_ge(s_st.h, den_dma[-1])
                    s_ld.dma(q.dma_start(out=dtile[:], in_=den_d[pr * 16:(pr + 1) * 16, :]))

                dt_ld = s_ld.v

                @blk.vector
                def _(v, pr=pr, dt_ld=dt_ld):
                    v.wait_ge(s_ld.h, dt_ld)
                    s_dv.inc(nc.vector.reciprocal(rdtile[:], dtile[:]))

                rd_dv = s_dv.v

                @blk.sync
                def _(q, pr=pr, rd_dv=rd_dv):
                    q.wait_ge(s_dv.h, rd_dv)
                    s_st.dma(q.dma_start(out=rden_d[pr * 16:(pr + 1) * 16, :], in_=rdtile[:]))

                rd_st = s_st.v

                for qc in range(NQC):
                    sl0 = 0
                    sl1 = 1

                    @blk.gpsimd
                    def _(g, pr=pr, qc=qc, sl0=sl0, sl1=sl1, rd_st=rd_st):
                        g.wait_ge(s_st.h, rd_st)
                        if qc >= 1 and (pr, qc - 1) in nrm_pe:
                            g.wait_ge(s_pe.h, nrm_pe[(pr, qc - 1)])
                        s_gld.dma(g.dma_start(
                            out=srow[0:1, sl0 * 512:(sl0 + 1) * 512],
                            in_=rden_d[2 * pr * 8 + qc:2 * pr * 8 + qc + 1, :]))
                        s_gld.dma(g.dma_start(
                            out=srow[0:1, sl1 * 512:(sl1 + 1) * 512],
                            in_=rden_d[(2 * pr + 1) * 8 + qc:(2 * pr + 1) * 8 + qc + 1, :]))

                    sr_gld = s_gld.v

                    @blk.tensor
                    def _(t, qc=qc, sl0=sl0, sl1=sl1, sr_gld=sr_gld):
                        t.wait_ge(s_gld.h, sr_gld)
                        t.wait_ge(s_dv.h, s_dv.v)
                        nc.tensor.matmul(
                            psD[0:64, :], onesb[0:1, 0:64],
                            srow[0:1, sl0 * 512:(sl0 + 1) * 512],
                            start=True, stop=True, tile_position=(0, 0))
                        instr = nc.tensor.matmul(
                            psD[64:128, :], onesb[0:1, 0:64],
                            srow[0:1, sl1 * 512:(sl1 + 1) * 512],
                            start=True, stop=True, tile_position=(0, 64))
                        nrm_pe[(pr, qc)] = s_pe.inc(instr)

                    nr_pe = s_pe.v

                    @blk.vector
                    def _(v, pr=pr, qc=qc, nr_pe=nr_pe):
                        v.wait_ge(s_pe.h, nr_pe)
                        s_dv.inc(nc.vector.tensor_tensor(
                            tmpn[:], attnU[:, qc * 512:(qc + 1) * 512], psD[:], op=OP.mult))
                        v.wait_ge(s_dv.h, s_dv.v)
                        s_dv.inc(nc.vector.tensor_tensor(
                            oa[:, pr, qc * 512:(qc + 1) * 512],
                            oa[:, pr, qc * 512:(qc + 1) * 512], tmpn[:], op=OP.add))

                nrm_dv.append(s_dv.v)

    try:
        _psF.close()
    except NameError:
        pass

    with nc.Block() as blk:
        @blk.sync
        def _(q):
            q.wait_ge(s_dv.h, s_dv.v)
            q.wait_ge(s_pe.h, s_pe.v)
            q.wait_ge(s_ac.h, s_ac.v)
            q.wait_ge(s_po.h, s_po.v)
            for g in range(4):
                s_st.dma(q.dma_start(out=out_d[:, g * S:(g + 1) * S],
                                     in_=out_acc[:, g * S:(g + 1) * S]))
            if debug:
                dap_st = nc.dram_tensor("dbg_state", [128, 64], F32, kind="ExternalOutput").ap()
                s_st.dma(q.dma_start(out=dap_st, in_=state[:]))
                for nm, src, shape, dt in [
                    ("xnup", xnup_d, (128, 4 * S), BF16),
                    ("comb", comb_d, (NE, S), F32),
                    ("bT", bT_d, (NST, S), F32),
                    ("cT", cT_d, (NST, S), F32),
                    ("qt", qt_d, (128, 4 * S), BF16),
                    ("den", den_d, (64, 512), F32),
                ]:
                    dap = nc.dram_tensor("dbg_" + nm, list(shape), dt,
                                         kind="ExternalOutput").ap()
                    s_st.dma(q.dma_start(out=dap, in_=src))
            q.wait_ge(s_st.h, s_st.v)

    return nc


def _prep_core_inputs(inputs, c, S):
    b, h = c // 2, c % 2
    own = slice(h * DH, (h + 1) * DH)
    perm = np.r_[h * DH:(h + 1) * DH, (1 - h) * DH:(2 - h) * DH]

    x = np.ascontiguousarray(inputs["L0_attn_x"][b, :S, :][:, perm], np.float32)

    Wd = inputs["L0_attn_Wd"][perm][:, own]
    Wb = inputs["L0_attn_Wb"][perm]
    Wc = inputs["L0_attn_Wc"][perm]
    Wg = inputs["L0_ffn_W_gate"][perm]
    Wqkv = inputs["L0_ffn_W_qkv"]
    Wq = Wqkv[perm][:, 0 * D:1 * D][:, own]
    Wk = Wqkv[perm][:, 1 * D:2 * D][:, own]
    Wv = Wqkv[perm][:, 2 * D:3 * D][:, own]
    bqkv = inputs["L0_ffn_b_qkv"]
    Wexp = inputs["L0_ffn_W_experts"][:, perm][:, :, own]
    bexp = inputs["L0_ffn_b_experts"][:, own]

    col4 = lambda v: np.ascontiguousarray(np.asarray(v, np.float32).reshape(4, 128).T)
    f8 = lambda w, n: np.ascontiguousarray(
        np.asarray(w, np.float32).reshape(8, 128, n).transpose(1, 0, 2))

    A_own = np.asarray(inputs["L0_attn_A_log"][own], np.float32)
    alog = np.ascontiguousarray(
        A_own.reshape(64, 8, NST).transpose(1, 2, 0).reshape(128, 64))

    zr = np.zeros((128, 2048), np.float32)
    for k16 in range(16):
        for p in range(128):
            zr[p, k16 * 128 + 8 * k16 + p // NST] = 1.0
    r1f = np.zeros((128, 2048), np.float32)
    for k16 in range(16):
        for p in range(128):
            r1f[8 * k16 + p // NST, k16 * 128 + p] = 1.0

    return dict(
        x=x,
        identf=np.eye(128, dtype=np.float32),
        identb=np.eye(128, dtype=ml_dtypes.bfloat16),
        onesb=np.ones((1, 128), ml_dtypes.bfloat16),
        zred=zr.astype(ml_dtypes.bfloat16),
        r1f=r1f.astype(ml_dtypes.bfloat16),
        alog=alog,
        dskip=col4(inputs["L0_attn_D_skip"][own]),
        wd=f8(Wd, DH), bd=col4(inputs["L0_attn_bd"][own]),
        wb=f8(Wb, NST),
        bb=np.ascontiguousarray(np.asarray(inputs["L0_attn_bb"], np.float32).reshape(NST, 1)),
        wc=f8(Wc, NST),
        bc=np.ascontiguousarray(np.asarray(inputs["L0_attn_bc"], np.float32).reshape(NST, 1)),
        wg=f8(Wg, NE),
        bg=np.ascontiguousarray(np.asarray(inputs["L0_ffn_b_gate"], np.float32).reshape(NE, 1)),
        wq=f8(Wq, DH), wk=f8(Wk, DH), wv=f8(Wv, DH),
        bq=col4(bqkv[0 * D:1 * D][own]),
        bk=col4(bqkv[1 * D:2 * D][own]),
        bv=col4(bqkv[2 * D:3 * D][own]),
        wexp=np.ascontiguousarray(
            np.asarray(Wexp, np.float32).reshape(NE, 8, 128, 4, 128).transpose(0, 2, 1, 3, 4)),
        bexp=np.ascontiguousarray(bexp, np.float32),
        cvals=np.ascontiguousarray(np.broadcast_to(
            np.asarray(CONST_VALS, np.float32), (128, 8))),
    )


def _assemble(results, S):
    out = np.empty((4, S, D), np.float32)
    for c in range(8):
        b, h = c // 2, c % 2
        shard = np.asarray(results[c]["out"])
        shard = shard.reshape(128, 4, S).transpose(1, 0, 2).reshape(DH, S)
        out[b, :, h * DH:(h + 1) * DH] = shard.T
    return out


def kernel(**inputs):
    S = inputs["L0_attn_x"].shape[1]
    nc = build_nc(S)
    in_maps = [_prep_core_inputs(inputs, c, S) for c in range(8)]
    res = run_bass_kernel_spmd(nc, in_maps, core_ids=list(range(8)))
    return _assemble(res.results, S)
